# version 01
# baseline (speedup 1.0000x reference)
import numpy as np
import ml_dtypes
import concourse.bass as bass
import concourse.mybir as mybir
from concourse.bass_utils import run_bass_kernel_spmd
from contextlib import ExitStack

F32 = mybir.dt.float32
BF16 = mybir.dt.bfloat16
AF = mybir.ActivationFunctionType
ALU = mybir.AluOpType
AX = mybir.AxisListType

B, S, D, FF, NE = 16, 2048, 1024, 2048, 8
NCORES = 8
SEQ_PER_CORE = B // NCORES
TOKENS = SEQ_PER_CORE * S
TG = 1024
NTG = S // TG
NDC = D // 128
NFC = FF // 128
NTC = TG // 128
LNT = S // 128
EPS = 1e-5

ENGINES = ("sync", "gpsimd", "vector", "scalar", "tensor")


class Sched:

    def __init__(self):
        self.ops = {e: [] for e in ENGINES}
        self.cnt = {}

    def op(self, eng, fn, waits=(), inc=None, inc_amt=1):
        mark = None
        if inc is not None:
            self.cnt[inc] = self.cnt.get(inc, 0) + inc_amt
            mark = self.cnt[inc]
        self.ops[eng].append((fn, tuple(w for w in waits if w[1] is not None), inc, inc_amt))
        return mark


def build_nc(debug=False):
    nc = bass.Bass("TRN2", target_bir_lowering=False)

    x_h = nc.dram_tensor("x", [TOKENS, D], F32, kind="ExternalInput").ap()
    wl_h = nc.dram_tensor("wl", [128, NDC], F32, kind="ExternalInput").ap()
    wr_h = nc.dram_tensor("wr", [128, NDC], F32, kind="ExternalInput").ap()
    rt_h = nc.dram_tensor("rt", [D, NE], F32, kind="ExternalInput").ap()
    wg_h = nc.dram_tensor("wg", [NE, D, FF], F32, kind="ExternalInput").ap()
    wu_h = nc.dram_tensor("wu", [NE, D, FF], F32, kind="ExternalInput").ap()
    wd_h = nc.dram_tensor("wd", [NE, FF, D], F32, kind="ExternalInput").ap()
    id_h = nc.dram_tensor("ident", [128, 128], BF16, kind="ExternalInput").ap()
    c8_h = nc.dram_tensor("c8", [128, NE], F32, kind="ExternalInput").ap()
    out_h = nc.dram_tensor("out", [TOKENS, D], F32, kind="ExternalOutput").ap()

    wgc = nc.dram_tensor("wgc", [NE, NFC, 128, D], BF16, kind="Internal").ap()
    wuc = nc.dram_tensor("wuc", [NE, NFC, 128, D], BF16, kind="Internal").ap()
    wdc = nc.dram_tensor("wdc", [NE, NFC, 128, D], BF16, kind="Internal").ap()

    if debug:
        dbg_xn = nc.dram_tensor("dbg_xn", [TOKENS, D], BF16, kind="ExternalOutput").ap()
        dbg_fw = nc.dram_tensor("dbg_fw", [SEQ_PER_CORE, 128, 128], F32, kind="ExternalOutput").ap()
        dbg_ls = nc.dram_tensor("dbg_ls", [SEQ_PER_CORE, 128, 64], F32, kind="ExternalOutput").ap()
        dbg_xf = nc.dram_tensor("dbg_xf", [SEQ_PER_CORE, 128, NDC * S], BF16, kind="ExternalOutput").ap()

    sch = Sched()
    SV, SS, ST, SO = "SV", "SS", "ST", "SO"
    SEM_NAMES = (["SX_C", "SX_X0", "SX_X1", "SX_A", "SX_D",
                  "SX_G0", "SX_G1", "SX_U0", "SX_U1", "SO", "SW_R", "SX_B",
                  "SV", "SS", "ST"]
                 + [f"SWb{e}{m}" for e in range(NE) for m in "gud"])

    with ExitStack() as ctx:
        E = ctx.enter_context
        xn_tm = E(nc.sbuf_tensor("xn_tm", [128, LNT * D], BF16))
        xn_fm = E(nc.sbuf_tensor("xn_fm", [128, NDC * S], BF16))
        acc = E(nc.sbuf_tensor("acc", [128, NTC * D], F32))
        h_sb = E(nc.sbuf_tensor("h_sb", [128, NFC * TG], BF16))
        wd_all = E(nc.sbuf_tensor("wd_all", [128, NFC * D], BF16))
        wg_buf = [E(nc.sbuf_tensor(f"wgb{i}", [128, D], BF16)) for i in range(2)]
        wu_buf = [E(nc.sbuf_tensor(f"wub{i}", [128, D], BF16)) for i in range(2)]
        unionA = E(nc.sbuf_tensor("unionA", [128, 5120], F32))
        xw_buf = [E(nc.sbuf_tensor(f"xwb{i}", [128, NDC * 128], BF16)) for i in range(2)]
        attn_sb = E(nc.sbuf_tensor("attn_sb", [128, S], BF16))
        attnT_sb = E(nc.sbuf_tensor("attnT_sb", [128, S], BF16))
        astat = E(nc.sbuf_tensor("astat", [128, 20], F32))
        lstat = E(nc.sbuf_tensor("lstat", [128, 12], F32))
        logits_sb = E(nc.sbuf_tensor("logits_sb", [128, 128], F32))
        fw_sb = E(nc.sbuf_tensor("fw_sb", [128, 128], F32))
        ra = E(nc.sbuf_tensor("ra", [128, 128], F32))
        rb = E(nc.sbuf_tensor("rb", [128, 128], F32))
        rc = E(nc.sbuf_tensor("rc", [128, 128], F32))
        rd = E(nc.sbuf_tensor("rd", [128, 128], F32))
        r16 = E(nc.sbuf_tensor("r16", [128, 288], F32))
        ident = E(nc.sbuf_tensor("ident_sb", [128, 128], BF16))
        c8 = E(nc.sbuf_tensor("c8_sb", [128, NE], F32))
        wl_sb = E(nc.sbuf_tensor("wl_sb", [128, NDC], F32))
        wr_sb = E(nc.sbuf_tensor("wr_sb", [128, NDC], F32))
        wlr = E(nc.sbuf_tensor("wlr", [128, NDC], F32))
        rt_bf = E(nc.sbuf_tensor("rt_bf", [128, NDC * NE], BF16))
        dls = E(nc.sbuf_tensor("dls", [128, 64], F32)) if debug else None

        x_t = [unionA[:, 0:1024], unionA[:, 1024:2048]]
        xc_t = [unionA[:, 2048:3072], unionA[:, 3072:4096]]
        sqdump = unionA[:, 4096:5120]
        sg_t = [unionA[:, 0:1024], unionA[:, 1024:2048]]

        P = [E(nc.psum_tensor(f"P{i}", [128, 512], F32)) for i in range(6)]
        T = [E(nc.psum_tensor(f"T{i}", [128, 1024], BF16)) for i in range(2)]

        sems = {n: E(nc.semaphore(n)) for n in SEM_NAMES}

        bankfree = {f"P{i}": None for i in range(6)}
        bankfree.update({"T0": None, "T1": None})

        def bwait(*names):
            return [bankfree[n] for n in names if bankfree[n] is not None]

        wg_busy = [None, None]
        wu_busy = [None, None]
        sg_busy = [None, None]
        h_guard = None
        wd_free = None
        xn_guard = None
        store_mark = {}
        last_hmult = {}

        sch.op("sync", lambda e: e.dma_start(out=ident[:], in_=id_h), inc="SX_C", inc_amt=16)
        sch.op("sync", lambda e: e.dma_start(out=c8[:], in_=c8_h), inc="SX_C", inc_amt=16)
        sch.op("sync", lambda e: e.dma_start(out=wl_sb[:], in_=wl_h), inc="SX_C", inc_amt=16)
        sch.op("sync", lambda e: e.dma_start(out=wr_sb[:], in_=wr_h), inc="SX_C", inc_amt=16)
        M_C = 64
        m_wlr = sch.op("vector", lambda e: e.tensor_tensor(wlr[:], wl_sb[:], wr_sb[:], op=ALU.mult),
                       waits=[("SX_C", M_C)], inc=SV)
        for dc in range(NDC):
            sch.op("gpsimd", lambda e, dc=dc: e.dma_start(
                out=rt_bf[:, dc * NE:(dc + 1) * NE], in_=rt_h[dc * 128:(dc + 1) * 128, :]),
                inc="SW_R", inc_amt=16)
        M_RT = 128

        build_done = {}
        for e_i in range(NE):
            for mat, src_h, dst_h in (("g", wg_h, wgc), ("u", wu_h, wuc)):
                sem = f"SWb{e_i}{mat}"
                m = None
                for dc in range(NDC):
                    for q in range(4):
                        fc0, fc1 = q * 4, (q + 1) * 4
                        m = sch.op("gpsimd", lambda e, e_i=e_i, dc=dc, fc0=fc0, fc1=fc1,
                                   src_h=src_h, dst_h=dst_h: e.dma_start(
                            out=dst_h[e_i, fc0:fc1, :, dc * 128:(dc + 1) * 128].rearrange("fc p f -> p fc f"),
                            in_=src_h[e_i, dc * 128:(dc + 1) * 128, fc0 * 128:fc1 * 128].rearrange("p (fc f) -> p fc f", f=128)),
                            inc=sem, inc_amt=16)
                build_done[(e_i, mat)] = (sem, m)
            sem = f"SWb{e_i}d"
            m = sch.op("gpsimd", lambda e, e_i=e_i: e.dma_start(
                out=wdc[e_i].rearrange("fc p dd -> (fc p) dd"), in_=wd_h[e_i]),
                inc=sem, inc_amt=16)
            build_done[(e_i, "d")] = (sem, m)

        for seq in range(SEQ_PER_CORE):
            sbase = seq * S
            xn_done = None
            m_xn_prev = None
            for tc in range(LNT):
                p = tc % 2
                mean = lstat[:, p:p + 1]
                rs = lstat[:, 2 + p:3 + p]
                ss = lstat[:, 4 + p:5 + p]
                sd = lstat[:, 6 + p:7 + p]
                rinv = lstat[:, 8 + p:9 + p]
                rinv2 = lstat[:, 10 + p:11 + p]
                w_ln = []
                if seq > 0 and tc < 2:
                    w_ln.append((SV, last_hmult[seq - 1]))
                if tc >= 2:
                    w_ln.append((SV, m_xn_prev))
                m_x = sch.op("sync", lambda e, tc=tc, p=p, sbase=sbase: e.dma_start(
                    out=x_t[p], in_=x_h[sbase + tc * 128: sbase + (tc + 1) * 128, :]),
                    waits=w_ln, inc=f"SX_X{p}", inc_amt=16)
                m_rs = sch.op("vector", lambda e, rs=rs, p=p: e.reduce_sum(rs, x_t[p], axis=AX.X),
                       waits=[(f"SX_X{p}", m_x)], inc=SV)
                m_mn = sch.op("scalar", lambda e, mean=mean, rs=rs: e.mul(mean, rs, 1.0 / D),
                       waits=[(SV, m_rs)], inc=SS)
                sch.op("vector", lambda e, p=p, mean=mean: e.tensor_scalar(
                    xc_t[p], x_t[p], mean, 0.0, op0=ALU.subtract, op1=ALU.add),
                    waits=[(SS, m_mn)])
                sch.op("vector", lambda e, p=p: e.tensor_tensor(
                    sqdump, xc_t[p], xc_t[p], op=ALU.mult))
                m_ss = sch.op("vector", lambda e, ss=ss: e.reduce_sum(ss, sqdump, axis=AX.X), inc=SV)
                m_sd = sch.op("scalar", lambda e, sd=sd, ss=ss: e.activation(
                    sd, ss, AF.Sqrt, bias=np.float32(EPS), scale=np.float32(1.0 / D)),
                    waits=[(SV, m_ss)], inc=SS)
                m_ri = sch.op("vector", lambda e, rinv=rinv, sd=sd: e.reciprocal(rinv, sd),
                       waits=[(SS, m_sd)], inc=SV)
                m_rb = sch.op("scalar", lambda e, rinv=rinv, rinv2=rinv2: e.copy(rinv2, rinv),
                       waits=[(SV, m_ri)], inc=SS)
                w_xn = [(SS, m_rb)]
                if seq > 0 and tc == 0:
                    w_xn.append((ST, xn_guard))
                if debug and seq > 0 and tc == 0:
                    w_xn.append(("SX_B", 64 * seq))
                m_xn = sch.op("vector", lambda e, p=p, tc=tc, rinv2=rinv2: e.tensor_scalar(
                    xn_tm[:, tc * D:(tc + 1) * D], xc_t[p], rinv2, 0.0,
                    op0=ALU.mult, op1=ALU.add), waits=w_xn, inc=SV)
                m_xn_prev = m_xn
                if debug:
                    sch.op("scalar", lambda e, tc=tc, mean=mean: e.copy(dls[:, tc:tc + 1], mean))
                    sch.op("scalar", lambda e, tc=tc, rinv2=rinv2: e.copy(dls[:, 32 + tc:33 + tc], rinv2))
                tb = f"T{p}"
                m_tr = None
                for dc in range(NDC):
                    m_tr = sch.op("tensor", lambda e, p=p, tc=tc, dc=dc: e.transpose(
                        T[p][:, dc * 128:(dc + 1) * 128],
                        xn_tm[:, tc * D + dc * 128: tc * D + (dc + 1) * 128], ident[:]),
                        waits=([(SV, m_xn), ("SX_C", M_C)] + (bwait(tb) if dc == 0 else [])),
                        inc=(ST if dc == NDC - 1 else None))
                m_ev = sch.op("vector", lambda e, p=p, tc=tc: e.tensor_copy(
                    xn_fm[:].rearrange("p (dc t) -> p dc t", dc=NDC)[:, :, tc * 128:(tc + 1) * 128],
                    T[p][:].rearrange("p (dc t) -> p dc t", dc=NDC)),
                    waits=[(ST, m_tr)], inc=SV)
                bankfree[tb] = (SV, m_ev)
                xn_done = m_ev
                m_lg = None
                for dc in range(NDC):
                    m_lg = sch.op("tensor", lambda e, tc=tc, dc=dc: e.matmul(
                        P[4][:, 0:NE],
                        xn_fm[:, dc * S + tc * 128: dc * S + (tc + 1) * 128],
                        rt_bf[:, dc * NE:(dc + 1) * NE],
                        start=(dc == 0), stop=(dc == NDC - 1)),
                        waits=([(SV, m_ev), ("SW_R", M_RT)] + (bwait("P4") if dc == 0 else [])),
                        inc=(ST if dc == NDC - 1 else None))
                m_lc = sch.op("scalar", lambda e, tc=tc: e.copy(
                    logits_sb[:, tc * NE:(tc + 1) * NE], P[4][:, 0:NE]),
                    waits=[(ST, m_lg)], inc=SS)
                bankfree["P4"] = (SS, m_lc)

            def a3(t):
                return t[:].rearrange("p (t e) -> p t e", e=NE)

            def bc16(ap16):
                return ap16.unsqueeze(2).broadcast_to([128, 16, NE])

            def bc8(ap8):
                return ap8.unsqueeze(1).broadcast_to([128, 16, NE])

            nmax = r16[:, 0:16]
            nmaxb = r16[:, 16:32]
            rsum = r16[:, 32:48]
            rsumb = r16[:, 48:64]
            rrec = r16[:, 64:80]
            rrecb = r16[:, 80:96]
            m1 = r16[:, 96:112]
            m1b = r16[:, 112:128]
            m2 = r16[:, 128:144]
            m2b = r16[:, 144:160]
            c1 = r16[:, 160:176]
            c1b = r16[:, 176:192]
            c2 = r16[:, 192:208]
            c2b = r16[:, 208:224]
            d21 = r16[:, 224:240]
            nw1 = r16[:, 240:256]
            nw2 = r16[:, 256:272]
            mv = sch.op("vector", lambda e: e.tensor_reduce(nmax, a3(logits_sb), axis=AX.X, op=ALU.max, negate=True),
                        waits=[(SS, m_lc)], inc=SV)
            ms = sch.op("scalar", lambda e: e.copy(nmaxb, nmax), waits=[(SV, mv)], inc=SS)
            mv = sch.op("vector", lambda e: e.tensor_tensor(a3(rb), a3(logits_sb), bc16(nmaxb), op=ALU.add),
                        waits=[(SS, ms)], inc=SV)
            ms = sch.op("scalar", lambda e: e.activation(ra[:], rb[:], AF.Exp), waits=[(SV, mv)], inc=SS)
            mv = sch.op("vector", lambda e: e.tensor_reduce(rsum, a3(ra), axis=AX.X, op=ALU.add),
                        waits=[(SS, ms)], inc=SV)
            ms = sch.op("scalar", lambda e: e.copy(rsumb, rsum), waits=[(SV, mv)], inc=SS)
            mv = sch.op("vector", lambda e: e.reciprocal(rrec, rsumb), waits=[(SS, ms)], inc=SV)
            ms = sch.op("scalar", lambda e: e.copy(rrecb, rrec), waits=[(SV, mv)], inc=SS)
            sch.op("vector", lambda e: e.tensor_tensor(a3(rb), a3(ra), bc16(rrecb), op=ALU.mult),
                   waits=[(SS, ms)])
            mv = sch.op("vector", lambda e: e.tensor_reduce(m1, a3(rb), axis=AX.X, op=ALU.max), inc=SV)
            ms = sch.op("scalar", lambda e: e.copy(m1b, m1), waits=[(SV, mv)], inc=SS)
            sch.op("vector", lambda e: e.tensor_tensor(a3(rc), a3(rb), bc16(m1b), op=ALU.is_ge),
                   waits=[(SS, ms)])
            sch.op("vector", lambda e: e.tensor_tensor(a3(rd), a3(rc), bc8(c8[:]), op=ALU.mult))
            mv = sch.op("vector", lambda e: e.tensor_reduce(c1, a3(rd), axis=AX.X, op=ALU.max), inc=SV)
            ms = sch.op("scalar", lambda e: e.copy(c1b, c1), waits=[(SV, mv)], inc=SS)
            sch.op("vector", lambda e: e.tensor_tensor(a3(rc), a3(rd), bc16(c1b), op=ALU.is_ge),
                   waits=[(SS, ms)])
            sch.op("vector", lambda e: e.scalar_tensor_tensor(a3(rd), a3(rc), -2.0, a3(rb), op0=ALU.mult, op1=ALU.add))
            mv = sch.op("vector", lambda e: e.tensor_reduce(m2, a3(rd), axis=AX.X, op=ALU.max), inc=SV)
            ms = sch.op("scalar", lambda e: e.copy(m2b, m2), waits=[(SV, mv)], inc=SS)
            sch.op("vector", lambda e: e.tensor_tensor(a3(ra), a3(rd), bc16(m2b), op=ALU.is_ge),
                   waits=[(SS, ms)])
            sch.op("vector", lambda e: e.tensor_tensor(a3(rd), a3(ra), bc8(c8[:]), op=ALU.mult))
            mv = sch.op("vector", lambda e: e.tensor_reduce(c2, a3(rd), axis=AX.X, op=ALU.max), inc=SV)
            ms = sch.op("scalar", lambda e: e.copy(c2b, c2), waits=[(SV, mv)], inc=SS)
            sch.op("vector", lambda e: e.tensor_tensor(a3(ra), a3(rd), bc16(c2b), op=ALU.is_ge),
                   waits=[(SS, ms)])
            mv = sch.op("vector", lambda e: e.tensor_tensor(d21, m2b, m1b, op=ALU.subtract), inc=SV)
            sch.op("scalar", lambda e: e.activation(nw2, d21, AF.Sigmoid), waits=[(SV, mv)])
            ms = sch.op("scalar", lambda e: e.activation(nw1, d21, AF.Sigmoid, scale=np.float32(-1.0)), inc=SS)
            sch.op("vector", lambda e: e.tensor_tensor(a3(rb), a3(rc), bc16(nw1), op=ALU.mult),
                   waits=[(SS, ms)])
            sch.op("vector", lambda e: e.tensor_tensor(a3(rd), a3(ra), bc16(nw2), op=ALU.mult))
            m_fw = sch.op("vector", lambda e: e.tensor_tensor(fw_sb[:], rb[:], rd[:], op=ALU.add), inc=SV)
            sg_busy = [xn_done, xn_done]

            if debug:
                sch.op("sync", lambda e, sbase=sbase: e.dma_start(
                    out=dbg_xn[sbase:sbase + S, :].rearrange("(tc p) d -> p tc d", p=128),
                    in_=xn_tm[:].rearrange("p (tc d) -> p tc d", tc=LNT)),
                    waits=[(SV, xn_done)], inc="SX_B", inc_amt=16)
                sch.op("sync", lambda e, seq=seq: e.dma_start(
                    out=dbg_fw[seq], in_=fw_sb[:]), waits=[(SV, m_fw)], inc="SX_B", inc_amt=16)
                sch.op("sync", lambda e, seq=seq: e.dma_start(
                    out=dbg_ls[seq], in_=dls[:]), inc="SX_B", inc_amt=16)
                sch.op("sync", lambda e, seq=seq: e.dma_start(
                    out=dbg_xf[seq], in_=xn_fm[:]), inc="SX_B", inc_amt=16)

            for tg in range(NTG):
                tgg = seq * NTG + tg
                tbase = sbase + tg * TG
                m_ai = None
                for tc in range(NTC):
                    w_ai = [(SO, store_mark.get(tgg - 1))] if tc == 0 else []
                    m_ai = sch.op("sync", lambda e, tc=tc, tbase=tbase: e.dma_start(
                        out=acc[:, tc * D:(tc + 1) * D],
                        in_=x_h[tbase + tc * 128: tbase + (tc + 1) * 128, :]),
                        waits=w_ai, inc="SX_A", inc_amt=16)

                for qc in range(NTC):
                    qt = tg * NTC + qc
                    pp = qc % 2
                    m_xw = None
                    for dc in range(NDC):
                        m_xw = sch.op("vector", lambda e, dc=dc, pp=pp, qt=qt: e.tensor_scalar(
                            xw_buf[pp][:, dc * 128:(dc + 1) * 128],
                            xn_fm[:, dc * S + qt * 128: dc * S + (qt + 1) * 128],
                            wlr[:, dc:dc + 1], 0.0, op0=ALU.mult, op1=ALU.add),
                            waits=[(SV, m_wlr), (SV, xn_done)],
                            inc=(SV if dc == NDC - 1 else None))
                    m_sc = None
                    for dc in range(NDC):
                        for kt in range(4):
                            m_sc = sch.op("tensor", lambda e, dc=dc, kt=kt, pp=pp: e.matmul(
                                P[kt][:],
                                xw_buf[pp][:, dc * 128:(dc + 1) * 128],
                                xn_fm[:, dc * S + kt * 512: dc * S + (kt + 1) * 512],
                                start=(dc == 0), stop=(dc == NDC - 1)),
                                waits=([(SV, m_xw)] + (bwait("P0", "P1", "P2", "P3") if dc == 0 else [])),
                                inc=(ST if (dc == NDC - 1 and kt == 3) else None))
                    m_m4 = None
                    for kt in range(4):
                        m_m4 = sch.op("vector", lambda e, kt=kt: e.reduce_max(
                            astat[:, kt:kt + 1], P[kt][:], axis=AX.X), waits=[(ST, m_sc)],
                            inc=(SV if kt == 3 else None))
                    m_m4b = sch.op("scalar", lambda e: e.copy(astat[:, 4:8], astat[:, 0:4]),
                                   waits=[(SV, m_m4)], inc=SS)
                    m_nm = sch.op("vector", lambda e: e.reduce_max(
                        astat[:, 8:9], astat[:, 4:8], axis=AX.X, negate=True),
                        waits=[(SS, m_m4b)], inc=SV)
                    m_exp = None
                    for kt in range(4):
                        m_exp = sch.op("scalar", lambda e, kt=kt: e.activation(
                            attn_sb[:, kt * 512:(kt + 1) * 512], P[kt][:], AF.Exp,
                            bias=astat[:, 8:9], accum_out=astat[:, 9 + kt:10 + kt]),
                            waits=[(SV, m_nm)], inc=(SS if kt == 3 else None))
                    for i in range(4):
                        bankfree[f"P{i}"] = (SS, m_exp)
                    m_su = sch.op("vector", lambda e: e.reduce_sum(
                        astat[:, 13:14], astat[:, 9:13], axis=AX.X), waits=[(SS, m_exp)], inc=SV)
                    m_sb = sch.op("scalar", lambda e: e.copy(astat[:, 14:15], astat[:, 13:14]),
                                  waits=[(SV, m_su)], inc=SS)
                    m_rc2 = sch.op("vector", lambda e: e.reciprocal(astat[:, 15:16], astat[:, 14:15]),
                                   waits=[(SS, m_sb)], inc=SV)
                    m_rec = sch.op("scalar", lambda e: e.copy(astat[:, 16:17], astat[:, 15:16]),
                                   waits=[(SV, m_rc2)], inc=SS)
                    m_tr = [None, None]
                    for half in range(2):
                        for j in range(8):
                            kc = half * 8 + j
                            m_tr[half] = sch.op("tensor", lambda e, half=half, j=j, kc=kc: e.transpose(
                                T[half][:, j * 128:(j + 1) * 128],
                                attn_sb[:, kc * 128:(kc + 1) * 128], ident[:]),
                                waits=([(SS, m_exp)] + (bwait(f"T{half}") if j == 0 else [])),
                                inc=(ST if j == 7 else None))
                    m_te = [None, None]
                    for half in range(2):
                        m_te[half] = sch.op("vector", lambda e, half=half: e.tensor_copy(
                            attnT_sb[:, half * 1024:(half + 1) * 1024], T[half][:]),
                            waits=[(ST, m_tr[half])], inc=SV)
                        bankfree[f"T{half}"] = (SV, m_te[half])
                    m_av = None
                    for kc in range(16):
                        for dh in range(2):
                            m_av = sch.op("tensor", lambda e, kc=kc, dh=dh: e.matmul(
                                P[4 + dh][:],
                                attnT_sb[:, kc * 128:(kc + 1) * 128],
                                xn_tm[:, kc * D + dh * 512: kc * D + (dh + 1) * 512],
                                start=(kc == 0), stop=(kc == 15)),
                                waits=([(SV, m_te[kc // 8])] + (bwait("P4", "P5") if kc == 0 else [])),
                                inc=(ST if (kc == 15 and dh == 1) else None))
                    m_ae = None
                    for dh in range(2):
                        m_ae = sch.op("vector", lambda e, dh=dh, qc=qc: e.scalar_tensor_tensor(
                            acc[:, qc * D + dh * 512: qc * D + (dh + 1) * 512],
                            P[4 + dh][:], astat[:, 16:17],
                            acc[:, qc * D + dh * 512: qc * D + (dh + 1) * 512],
                            op0=ALU.mult, op1=ALU.add),
                            waits=[(ST, m_av), (SS, m_rec), ("SX_A", m_ai)], inc=SV)
                    bankfree["P4"] = (SV, m_ae)
                    bankfree["P5"] = (SV, m_ae)

                for e_i in range(NE):
                    m_wg = {}
                    m_wu = {}

                    def load_wg(fc):
                        pp = fc % 2
                        m_wg[fc] = sch.op("sync", lambda e, e_i=e_i, fc=fc, pp=pp: e.dma_start(
                            out=wg_buf[pp][:], in_=wgc[e_i, fc]),
                            waits=[build_done[(e_i, "g")], (ST, wg_busy[pp])],
                            inc=f"SX_G{pp}", inc_amt=16)

                    def load_wu(fc):
                        pp = fc % 2
                        m_wu[fc] = sch.op("sync", lambda e, e_i=e_i, fc=fc, pp=pp: e.dma_start(
                            out=wu_buf[pp][:], in_=wuc[e_i, fc]),
                            waits=[build_done[(e_i, "u")], (ST, wu_busy[pp])],
                            inc=f"SX_U{pp}", inc_amt=16)

                    for fc in (0, 1):
                        load_wg(fc)
                        load_wu(fc)
                    m_wd = None
                    for fc in range(NFC):
                        m_wd = sch.op("sync", lambda e, e_i=e_i, fc=fc: e.dma_start(
                            out=wd_all[:, fc * D:(fc + 1) * D], in_=wdc[e_i, fc]),
                            waits=[build_done[(e_i, "d")]] + ([(ST, wd_free)] if fc == 0 else []),
                            inc="SX_D", inc_amt=16)

                    m_hm_last = None
                    for fc in range(NFC):
                        pp = fc % 2
                        gpair = (0, 1) if fc % 2 == 0 else (2, 3)
                        upair = (2, 3) if fc % 2 == 0 else (0, 1)
                        m_g = None
                        for dc in range(NDC):
                            for tt in range(2):
                                m_g = sch.op("tensor", lambda e, dc=dc, tt=tt, pp=pp, gpair=gpair, tg=tg: e.matmul(
                                    P[gpair[tt]][:],
                                    wg_buf[pp][:, dc * 128:(dc + 1) * 128],
                                    xn_fm[:, dc * S + tg * TG + tt * 512: dc * S + tg * TG + (tt + 1) * 512],
                                    start=(dc == 0), stop=(dc == NDC - 1)),
                                    waits=([(f"SX_G{pp}", m_wg[fc])] + (bwait(f"P{gpair[0]}", f"P{gpair[1]}") if dc == 0 else [])),
                                    inc=(ST if (dc == NDC - 1 and tt == 1) else None))
                        wg_busy[pp] = m_g
                        if fc + 2 < NFC:
                            load_wg(fc + 2)
                        m_si = None
                        for tt in range(2):
                            m_si = sch.op("scalar", lambda e, tt=tt, pp=pp, gpair=gpair: e.activation(
                                sg_t[pp][:, tt * 512:(tt + 1) * 512], P[gpair[tt]][:], AF.Silu),
                                waits=[(ST, m_g), (SV, sg_busy[pp])],
                                inc=(SS if tt == 1 else None))
                        bankfree[f"P{gpair[0]}"] = (SS, m_si)
                        bankfree[f"P{gpair[1]}"] = (SS, m_si)
                        m_u = None
                        for dc in range(NDC):
                            for tt in range(2):
                                m_u = sch.op("tensor", lambda e, dc=dc, tt=tt, pp=pp, upair=upair, tg=tg: e.matmul(
                                    P[upair[tt]][:],
                                    wu_buf[pp][:, dc * 128:(dc + 1) * 128],
                                    xn_fm[:, dc * S + tg * TG + tt * 512: dc * S + tg * TG + (tt + 1) * 512],
                                    start=(dc == 0), stop=(dc == NDC - 1)),
                                    waits=([(f"SX_U{pp}", m_wu[fc])] + (bwait(f"P{upair[0]}", f"P{upair[1]}") if dc == 0 else [])),
                                    inc=(ST if (dc == NDC - 1 and tt == 1) else None))
                        wu_busy[pp] = m_u
                        if fc + 2 < NFC:
                            load_wu(fc + 2)
                        m_hm = None
                        w_h = [(ST, h_guard)] if (h_guard is not None and fc == 0) else []
                        for tt in range(2):
                            m_hm = sch.op("vector", lambda e, tt=tt, pp=pp, fc=fc, upair=upair: e.tensor_tensor(
                                h_sb[:, fc * TG + tt * 512: fc * TG + (tt + 1) * 512],
                                sg_t[pp][:, tt * 512:(tt + 1) * 512], P[upair[tt]][:], op=ALU.mult),
                                waits=[(SS, m_si), (ST, m_u)] + (w_h if tt == 0 else []),
                                inc=(SV if tt == 1 else None))
                        sg_busy[pp] = m_hm
                        bankfree[f"P{upair[0]}"] = (SV, m_hm)
                        bankfree[f"P{upair[1]}"] = (SV, m_hm)
                        m_hm_last = m_hm
                    last_hmult[seq] = m_hm_last

                    m_dn_last = None
                    m_de = None
                    for tc in range(NTC):
                        dpair = (4, 5) if tc % 2 == 0 else (0, 1)
                        m_d = None
                        for fc in range(NFC):
                            for ddh in range(2):
                                m_d = sch.op("tensor", lambda e, fc=fc, ddh=ddh, tc=tc, dpair=dpair: e.matmul(
                                    P[dpair[ddh]][:],
                                    h_sb[:, fc * TG + tc * 128: fc * TG + (tc + 1) * 128],
                                    wd_all[:, fc * D + ddh * 512: fc * D + (ddh + 1) * 512],
                                    start=(fc == 0), stop=(fc == NFC - 1)),
                                    waits=([(SV, m_hm_last), ("SX_D", m_wd)] + (bwait(f"P{dpair[0]}", f"P{dpair[1]}") if fc == 0 else [])),
                                    inc=(ST if (fc == NFC - 1 and ddh == 1) else None))
                        m_dn_last = m_d
                        fcol = (tg * NTC + tc) * NE + e_i
                        for ddh in range(2):
                            m_de = sch.op("vector", lambda e, ddh=ddh, tc=tc, dpair=dpair, fcol=fcol: e.scalar_tensor_tensor(
                                acc[:, tc * D + ddh * 512: tc * D + (ddh + 1) * 512],
                                P[dpair[ddh]][:], fw_sb[:, fcol:fcol + 1],
                                acc[:, tc * D + ddh * 512: tc * D + (ddh + 1) * 512],
                                op0=ALU.mult, op1=ALU.add),
                                waits=[(ST, m_d), (SV, m_fw)], inc=SV)
                        bankfree[f"P{dpair[0]}"] = (SV, m_de)
                        bankfree[f"P{dpair[1]}"] = (SV, m_de)
                    h_guard = m_dn_last
                    wd_free = m_dn_last
                    xn_guard = m_dn_last
                    acc_done = m_de

                m_st = None
                for tc in range(NTC):
                    m_st = sch.op("sync", lambda e, tc=tc, tbase=tbase: e.dma_start(
                        out=out_h[tbase + tc * 128: tbase + (tc + 1) * 128, :],
                        in_=acc[:, tc * D:(tc + 1) * D]),
                        waits=[(SV, acc_done)], inc=SO, inc_amt=16)
                store_mark[tgg] = m_st

        total_so = sch.cnt[SO]
        sch.op("sync", lambda e: e.nop(), waits=[(SO, total_so)])

        with nc.Block() as block:
            for eng_name in ENGINES:
                ops = sch.ops[eng_name]

                def make(ops):
                    def emitter(eng):
                        waited = {}
                        for fn, waits, inc, inc_amt in ops:
                            for s, v in waits:
                                if waited.get(s, 0) < v:
                                    eng.wait_ge(sems[s], v)
                                    waited[s] = v
                            inst = fn(eng)
                            if inc is not None:
                                inst.then_inc(sems[inc], inc_amt)
                    return emitter

                getattr(block, eng_name)(make(ops))

    return nc


_NC_CACHE = {}


def _get_nc(debug=False):
    if debug not in _NC_CACHE:
        _NC_CACHE[debug] = build_nc(debug)
    return _NC_CACHE[debug]


def make_inputs(L0_attn_x, L0_attn_w_lhs, L0_attn_w_rhs, L0_ffn_router_kernel_DE,
                L0_ffn_kernel_gating_EDF, L0_ffn_kernel_up_EDF, L0_ffn_kernel_down_EFD):
    x = np.ascontiguousarray(np.asarray(L0_attn_x, dtype=np.float32))
    wl = np.ascontiguousarray(np.asarray(L0_attn_w_lhs, dtype=np.float32).reshape(NDC, 128).T)
    wr = np.ascontiguousarray(np.asarray(L0_attn_w_rhs, dtype=np.float32).reshape(NDC, 128).T)
    rt = np.ascontiguousarray(np.asarray(L0_ffn_router_kernel_DE, dtype=np.float32))
    wg = np.ascontiguousarray(np.asarray(L0_ffn_kernel_gating_EDF, dtype=np.float32))
    wu = np.ascontiguousarray(np.asarray(L0_ffn_kernel_up_EDF, dtype=np.float32))
    wd = np.ascontiguousarray(np.asarray(L0_ffn_kernel_down_EFD, dtype=np.float32))
    ident = np.eye(128, dtype=ml_dtypes.bfloat16)
    c8 = np.broadcast_to((NE - np.arange(NE, dtype=np.float32)), (128, NE)).copy()
    in_maps = []
    for c in range(NCORES):
        xs = x[c * SEQ_PER_CORE:(c + 1) * SEQ_PER_CORE].reshape(TOKENS, D)
        in_maps.append(dict(x=np.ascontiguousarray(xs), wl=wl, wr=wr, rt=rt,
                            wg=wg, wu=wu, wd=wd, ident=ident, c8=c8))
    return in_maps


def kernel(L0_attn_x, L0_attn_w_lhs, L0_attn_w_rhs, L0_ffn_router_kernel_DE,
           L0_ffn_kernel_gating_EDF, L0_ffn_kernel_up_EDF, L0_ffn_kernel_down_EFD):
    in_maps = make_inputs(L0_attn_x, L0_attn_w_lhs, L0_attn_w_rhs, L0_ffn_router_kernel_DE,
                          L0_ffn_kernel_gating_EDF, L0_ffn_kernel_up_EDF, L0_ffn_kernel_down_EFD)
    nc = _get_nc(debug=False)
    res = run_bass_kernel_spmd(nc, in_maps, core_ids=list(range(NCORES)))
    out = np.concatenate([np.asarray(r["out"], dtype=np.float32).reshape(SEQ_PER_CORE, S, D)
                          for r in res.results], axis=0)
    return out


# version 02
# speedup vs baseline: 1.1021x; 1.1021x over previous
import numpy as np
import ml_dtypes
import concourse.bass as bass
import concourse.mybir as mybir
from concourse.bass_utils import run_bass_kernel_spmd
from contextlib import ExitStack

F32 = mybir.dt.float32
BF16 = mybir.dt.bfloat16
AF = mybir.ActivationFunctionType
ALU = mybir.AluOpType
AX = mybir.AxisListType

B, S, D, FF, NE = 16, 2048, 1024, 2048, 8
NCORES = 8
SEQ_PER_CORE = B // NCORES
TOKENS = SEQ_PER_CORE * S
TG = 1024
NTG = S // TG
NDC = D // 128
NFC = FF // 128
NTC = TG // 128
LNT = S // 128
EPS = 1e-5

ENGINES = ("sync", "gpsimd", "vector", "scalar", "tensor")


class Sched:

    def __init__(self):
        self.ops = {e: [] for e in ENGINES}
        self.cnt = {}

    def op(self, eng, fn, waits=(), inc=None, inc_amt=1):
        mark = None
        if inc is not None:
            self.cnt[inc] = self.cnt.get(inc, 0) + inc_amt
            mark = self.cnt[inc]
        self.ops[eng].append((fn, tuple(w for w in waits if w[1] is not None), inc, inc_amt))
        return mark


def build_nc(debug=False):
    nc = bass.Bass("TRN2", target_bir_lowering=False)

    x_h = nc.dram_tensor("x", [TOKENS, D], F32, kind="ExternalInput").ap()
    wl_h = nc.dram_tensor("wl", [128, NDC], F32, kind="ExternalInput").ap()
    wr_h = nc.dram_tensor("wr", [128, NDC], F32, kind="ExternalInput").ap()
    rt_h = nc.dram_tensor("rt", [D, NE], F32, kind="ExternalInput").ap()
    wg_h = nc.dram_tensor("wg", [NE, D, FF], F32, kind="ExternalInput").ap()
    wu_h = nc.dram_tensor("wu", [NE, D, FF], F32, kind="ExternalInput").ap()
    wd_h = nc.dram_tensor("wd", [NE, FF, D], F32, kind="ExternalInput").ap()
    id_h = nc.dram_tensor("ident", [128, 128], BF16, kind="ExternalInput").ap()
    c8_h = nc.dram_tensor("c8", [128, NE], F32, kind="ExternalInput").ap()
    out_h = nc.dram_tensor("out", [TOKENS, D], F32, kind="ExternalOutput").ap()

    wgc = nc.dram_tensor("wgc", [NE, NFC, 128, D], BF16, kind="Internal").ap()
    wuc = nc.dram_tensor("wuc", [NE, NFC, 128, D], BF16, kind="Internal").ap()
    wdc = nc.dram_tensor("wdc", [NE, NFC, 128, D], BF16, kind="Internal").ap()

    if debug:
        dbg_xn = nc.dram_tensor("dbg_xn", [TOKENS, D], BF16, kind="ExternalOutput").ap()
        dbg_fw = nc.dram_tensor("dbg_fw", [SEQ_PER_CORE, 128, 128], F32, kind="ExternalOutput").ap()
        dbg_ls = nc.dram_tensor("dbg_ls", [SEQ_PER_CORE, 128, 64], F32, kind="ExternalOutput").ap()
        dbg_xf = nc.dram_tensor("dbg_xf", [SEQ_PER_CORE, 128, NDC * S], BF16, kind="ExternalOutput").ap()

    sch = Sched()
    SV, SS, ST, SO = "SV", "SS", "ST", "SO"
    SEM_NAMES = (["SX_C", "SX_X0", "SX_X1", "SX_A", "SX_D",
                  "SX_G0", "SX_G1", "SX_U0", "SX_U1", "SO", "SW_R", "SX_B",
                  "SV", "SS", "ST"]
                 + [f"SWb{e}{m}" for e in range(NE) for m in "gud"])

    with ExitStack() as ctx:
        E = ctx.enter_context
        xn_tm = E(nc.sbuf_tensor("xn_tm", [128, LNT * D], BF16))
        xn_fm = E(nc.sbuf_tensor("xn_fm", [128, NDC * S], BF16))
        acc = E(nc.sbuf_tensor("acc", [128, NTC * D], F32))
        h_sb = E(nc.sbuf_tensor("h_sb", [128, NFC * TG], BF16))
        wd_all = E(nc.sbuf_tensor("wd_all", [128, NFC * D], BF16))
        wg_buf = [E(nc.sbuf_tensor(f"wgb{i}", [128, D], BF16)) for i in range(2)]
        wu_buf = [E(nc.sbuf_tensor(f"wub{i}", [128, D], BF16)) for i in range(2)]
        unionA = E(nc.sbuf_tensor("unionA", [128, 5120], F32))
        xw_buf = [E(nc.sbuf_tensor(f"xwb{i}", [128, NDC * 128], BF16)) for i in range(2)]
        attn_sb = E(nc.sbuf_tensor("attn_sb", [128, S], BF16))
        attnT_sb = E(nc.sbuf_tensor("attnT_sb", [128, S], BF16))
        astat = E(nc.sbuf_tensor("astat", [128, 20], F32))
        lstat = E(nc.sbuf_tensor("lstat", [128, 12], F32))
        logits_sb = E(nc.sbuf_tensor("logits_sb", [128, 128], F32))
        fw_sb = E(nc.sbuf_tensor("fw_sb", [128, 128], F32))
        ra = E(nc.sbuf_tensor("ra", [128, 128], F32))
        rb = E(nc.sbuf_tensor("rb", [128, 128], F32))
        rc = E(nc.sbuf_tensor("rc", [128, 128], F32))
        rd = E(nc.sbuf_tensor("rd", [128, 128], F32))
        r16 = E(nc.sbuf_tensor("r16", [128, 288], F32))
        ident = E(nc.sbuf_tensor("ident_sb", [128, 128], BF16))
        c8 = E(nc.sbuf_tensor("c8_sb", [128, NE], F32))
        wl_sb = E(nc.sbuf_tensor("wl_sb", [128, NDC], F32))
        wr_sb = E(nc.sbuf_tensor("wr_sb", [128, NDC], F32))
        wlr = E(nc.sbuf_tensor("wlr", [128, NDC], F32))
        rt_bf = E(nc.sbuf_tensor("rt_bf", [128, NDC * NE], BF16))
        dls = E(nc.sbuf_tensor("dls", [128, 64], F32)) if debug else None

        x_t = [unionA[:, 0:1024], unionA[:, 1024:2048]]
        xc_t = [unionA[:, 2048:3072], unionA[:, 3072:4096]]
        sqdump = unionA[:, 4096:5120]
        sg_t = [unionA[:, 0:1024], unionA[:, 1024:2048]]

        P = [E(nc.psum_tensor(f"P{i}", [128, 512], F32)) for i in range(6)]
        T = [E(nc.psum_tensor(f"T{i}", [128, 1024], BF16)) for i in range(2)]

        sems = {n: E(nc.semaphore(n)) for n in SEM_NAMES}

        bankfree = {f"P{i}": None for i in range(6)}
        bankfree.update({"T0": None, "T1": None})

        def bwait(*names):
            return [bankfree[n] for n in names if bankfree[n] is not None]

        wg_busy = [None, None]
        wu_busy = [None, None]
        sg_busy = [None, None]
        h_guard = None
        wd_free = None
        xn_guard = None
        store_mark = {}
        last_hmult = {}

        sch.op("sync", lambda e: e.dma_start(out=ident[:], in_=id_h), inc="SX_C", inc_amt=16)
        sch.op("sync", lambda e: e.dma_start(out=c8[:], in_=c8_h), inc="SX_C", inc_amt=16)
        sch.op("sync", lambda e: e.dma_start(out=wl_sb[:], in_=wl_h), inc="SX_C", inc_amt=16)
        sch.op("sync", lambda e: e.dma_start(out=wr_sb[:], in_=wr_h), inc="SX_C", inc_amt=16)
        M_C = 64
        m_wlr = sch.op("vector", lambda e: e.tensor_tensor(wlr[:], wl_sb[:], wr_sb[:], op=ALU.mult),
                       waits=[("SX_C", M_C)], inc=SV)
        for dc in range(NDC):
            sch.op("gpsimd", lambda e, dc=dc: e.dma_start(
                out=rt_bf[:, dc * NE:(dc + 1) * NE], in_=rt_h[dc * 128:(dc + 1) * 128, :]),
                inc="SW_R", inc_amt=16)
        M_RT = 128

        build_done = {}
        for e_i in range(NE):
            for mat, src_h, dst_h in (("g", wg_h, wgc), ("u", wu_h, wuc)):
                sem = f"SWb{e_i}{mat}"
                m = None
                for dc in range(NDC):
                    for q in range(4):
                        fc0, fc1 = q * 4, (q + 1) * 4
                        m = sch.op("gpsimd", lambda e, e_i=e_i, dc=dc, fc0=fc0, fc1=fc1,
                                   src_h=src_h, dst_h=dst_h: e.dma_start(
                            out=dst_h[e_i, fc0:fc1, :, dc * 128:(dc + 1) * 128].rearrange("fc p f -> p fc f"),
                            in_=src_h[e_i, dc * 128:(dc + 1) * 128, fc0 * 128:fc1 * 128].rearrange("p (fc f) -> p fc f", f=128)),
                            inc=sem, inc_amt=16)
                build_done[(e_i, mat)] = (sem, m)
            sem = f"SWb{e_i}d"
            m = sch.op("gpsimd", lambda e, e_i=e_i: e.dma_start(
                out=wdc[e_i].rearrange("fc p dd -> (fc p) dd"), in_=wd_h[e_i]),
                inc=sem, inc_amt=16)
            build_done[(e_i, "d")] = (sem, m)

        for seq in range(SEQ_PER_CORE):
            sbase = seq * S
            xn_done = None
            m_xn_prev = None
            for tc in range(LNT):
                p = tc % 2
                mean = lstat[:, p:p + 1]
                rs = lstat[:, 2 + p:3 + p]
                ss = lstat[:, 4 + p:5 + p]
                sd = lstat[:, 6 + p:7 + p]
                rinv = lstat[:, 8 + p:9 + p]
                rinv2 = lstat[:, 10 + p:11 + p]
                w_ln = []
                if seq > 0 and tc < 2:
                    w_ln.append((SV, last_hmult[seq - 1]))
                if tc >= 2:
                    w_ln.append((SV, m_xn_prev))
                m_x = sch.op("sync", lambda e, tc=tc, p=p, sbase=sbase: e.dma_start(
                    out=x_t[p], in_=x_h[sbase + tc * 128: sbase + (tc + 1) * 128, :]),
                    waits=w_ln, inc=f"SX_X{p}", inc_amt=16)
                m_rs = sch.op("vector", lambda e, rs=rs, p=p: e.reduce_sum(rs, x_t[p], axis=AX.X),
                       waits=[(f"SX_X{p}", m_x)], inc=SV)
                m_mn = sch.op("scalar", lambda e, mean=mean, rs=rs: e.mul(mean, rs, 1.0 / D),
                       waits=[(SV, m_rs)], inc=SS)
                sch.op("vector", lambda e, p=p, mean=mean: e.tensor_scalar(
                    xc_t[p], x_t[p], mean, 0.0, op0=ALU.subtract, op1=ALU.add),
                    waits=[(SS, m_mn)])
                sch.op("vector", lambda e, p=p: e.tensor_tensor(
                    sqdump, xc_t[p], xc_t[p], op=ALU.mult))
                m_ss = sch.op("vector", lambda e, ss=ss: e.reduce_sum(ss, sqdump, axis=AX.X), inc=SV)
                m_sd = sch.op("scalar", lambda e, sd=sd, ss=ss: e.activation(
                    sd, ss, AF.Sqrt, bias=np.float32(EPS), scale=np.float32(1.0 / D)),
                    waits=[(SV, m_ss)], inc=SS)
                m_ri = sch.op("vector", lambda e, rinv=rinv, sd=sd: e.reciprocal(rinv, sd),
                       waits=[(SS, m_sd)], inc=SV)
                m_rb = sch.op("scalar", lambda e, rinv=rinv, rinv2=rinv2: e.copy(rinv2, rinv),
                       waits=[(SV, m_ri)], inc=SS)
                w_xn = [(SS, m_rb)]
                if seq > 0 and tc == 0:
                    w_xn.append((ST, xn_guard))
                if debug and seq > 0 and tc == 0:
                    w_xn.append(("SX_B", 64 * seq))
                m_xn = sch.op("vector", lambda e, p=p, tc=tc, rinv2=rinv2: e.tensor_scalar(
                    xn_tm[:, tc * D:(tc + 1) * D], xc_t[p], rinv2, 0.0,
                    op0=ALU.mult, op1=ALU.add), waits=w_xn, inc=SV)
                m_xn_prev = m_xn
                if debug:
                    sch.op("scalar", lambda e, tc=tc, mean=mean: e.copy(dls[:, tc:tc + 1], mean))
                    sch.op("scalar", lambda e, tc=tc, rinv2=rinv2: e.copy(dls[:, 32 + tc:33 + tc], rinv2))
                tb = f"T{p}"
                m_tr = None
                for dc in range(NDC):
                    m_tr = sch.op("tensor", lambda e, p=p, tc=tc, dc=dc: e.transpose(
                        T[p][:, dc * 128:(dc + 1) * 128],
                        xn_tm[:, tc * D + dc * 128: tc * D + (dc + 1) * 128], ident[:]),
                        waits=([(SV, m_xn), ("SX_C", M_C)] + (bwait(tb) if dc == 0 else [])),
                        inc=(ST if dc == NDC - 1 else None))
                m_ev = sch.op("vector", lambda e, p=p, tc=tc: e.tensor_copy(
                    xn_fm[:].rearrange("p (dc t) -> p dc t", dc=NDC)[:, :, tc * 128:(tc + 1) * 128],
                    T[p][:].rearrange("p (dc t) -> p dc t", dc=NDC)),
                    waits=[(ST, m_tr)], inc=SV)
                bankfree[tb] = (SV, m_ev)
                xn_done = m_ev
                m_lg = None
                for dc in range(NDC):
                    m_lg = sch.op("tensor", lambda e, tc=tc, dc=dc: e.matmul(
                        P[4][:, 0:NE],
                        xn_fm[:, dc * S + tc * 128: dc * S + (tc + 1) * 128],
                        rt_bf[:, dc * NE:(dc + 1) * NE],
                        start=(dc == 0), stop=(dc == NDC - 1)),
                        waits=([(SV, m_ev), ("SW_R", M_RT)] + (bwait("P4") if dc == 0 else [])),
                        inc=(ST if dc == NDC - 1 else None))
                m_lc = sch.op("scalar", lambda e, tc=tc: e.copy(
                    logits_sb[:, tc * NE:(tc + 1) * NE], P[4][:, 0:NE]),
                    waits=[(ST, m_lg)], inc=SS)
                bankfree["P4"] = (SS, m_lc)

            def a3(t):
                return t[:].rearrange("p (t e) -> p t e", e=NE)

            def bc16(ap16):
                return ap16.unsqueeze(2).broadcast_to([128, 16, NE])

            def bc8(ap8):
                return ap8.unsqueeze(1).broadcast_to([128, 16, NE])

            nmax = r16[:, 0:16]
            nmaxb = r16[:, 16:32]
            rsum = r16[:, 32:48]
            rsumb = r16[:, 48:64]
            rrec = r16[:, 64:80]
            rrecb = r16[:, 80:96]
            m1 = r16[:, 96:112]
            m1b = r16[:, 112:128]
            m2 = r16[:, 128:144]
            m2b = r16[:, 144:160]
            c1 = r16[:, 160:176]
            c1b = r16[:, 176:192]
            c2 = r16[:, 192:208]
            c2b = r16[:, 208:224]
            d21 = r16[:, 224:240]
            nw1 = r16[:, 240:256]
            nw2 = r16[:, 256:272]
            mv = sch.op("vector", lambda e: e.tensor_reduce(nmax, a3(logits_sb), axis=AX.X, op=ALU.max, negate=True),
                        waits=[(SS, m_lc)], inc=SV)
            ms = sch.op("scalar", lambda e: e.copy(nmaxb, nmax), waits=[(SV, mv)], inc=SS)
            mv = sch.op("vector", lambda e: e.tensor_tensor(a3(rb), a3(logits_sb), bc16(nmaxb), op=ALU.add),
                        waits=[(SS, ms)], inc=SV)
            ms = sch.op("scalar", lambda e: e.activation(ra[:], rb[:], AF.Exp), waits=[(SV, mv)], inc=SS)
            mv = sch.op("vector", lambda e: e.tensor_reduce(rsum, a3(ra), axis=AX.X, op=ALU.add),
                        waits=[(SS, ms)], inc=SV)
            ms = sch.op("scalar", lambda e: e.copy(rsumb, rsum), waits=[(SV, mv)], inc=SS)
            mv = sch.op("vector", lambda e: e.reciprocal(rrec, rsumb), waits=[(SS, ms)], inc=SV)
            ms = sch.op("scalar", lambda e: e.copy(rrecb, rrec), waits=[(SV, mv)], inc=SS)
            sch.op("vector", lambda e: e.tensor_tensor(a3(rb), a3(ra), bc16(rrecb), op=ALU.mult),
                   waits=[(SS, ms)])
            mv = sch.op("vector", lambda e: e.tensor_reduce(m1, a3(rb), axis=AX.X, op=ALU.max), inc=SV)
            ms = sch.op("scalar", lambda e: e.copy(m1b, m1), waits=[(SV, mv)], inc=SS)
            sch.op("vector", lambda e: e.tensor_tensor(a3(rc), a3(rb), bc16(m1b), op=ALU.is_ge),
                   waits=[(SS, ms)])
            sch.op("vector", lambda e: e.tensor_tensor(a3(rd), a3(rc), bc8(c8[:]), op=ALU.mult))
            mv = sch.op("vector", lambda e: e.tensor_reduce(c1, a3(rd), axis=AX.X, op=ALU.max), inc=SV)
            ms = sch.op("scalar", lambda e: e.copy(c1b, c1), waits=[(SV, mv)], inc=SS)
            sch.op("vector", lambda e: e.tensor_tensor(a3(rc), a3(rd), bc16(c1b), op=ALU.is_ge),
                   waits=[(SS, ms)])
            sch.op("vector", lambda e: e.scalar_tensor_tensor(a3(rd), a3(rc), -2.0, a3(rb), op0=ALU.mult, op1=ALU.add))
            mv = sch.op("vector", lambda e: e.tensor_reduce(m2, a3(rd), axis=AX.X, op=ALU.max), inc=SV)
            ms = sch.op("scalar", lambda e: e.copy(m2b, m2), waits=[(SV, mv)], inc=SS)
            sch.op("vector", lambda e: e.tensor_tensor(a3(ra), a3(rd), bc16(m2b), op=ALU.is_ge),
                   waits=[(SS, ms)])
            sch.op("vector", lambda e: e.tensor_tensor(a3(rd), a3(ra), bc8(c8[:]), op=ALU.mult))
            mv = sch.op("vector", lambda e: e.tensor_reduce(c2, a3(rd), axis=AX.X, op=ALU.max), inc=SV)
            ms = sch.op("scalar", lambda e: e.copy(c2b, c2), waits=[(SV, mv)], inc=SS)
            sch.op("vector", lambda e: e.tensor_tensor(a3(ra), a3(rd), bc16(c2b), op=ALU.is_ge),
                   waits=[(SS, ms)])
            mv = sch.op("vector", lambda e: e.tensor_tensor(d21, m2b, m1b, op=ALU.subtract), inc=SV)
            sch.op("scalar", lambda e: e.activation(nw2, d21, AF.Sigmoid), waits=[(SV, mv)])
            ms = sch.op("scalar", lambda e: e.activation(nw1, d21, AF.Sigmoid, scale=np.float32(-1.0)), inc=SS)
            sch.op("vector", lambda e: e.tensor_tensor(a3(rb), a3(rc), bc16(nw1), op=ALU.mult),
                   waits=[(SS, ms)])
            sch.op("vector", lambda e: e.tensor_tensor(a3(rd), a3(ra), bc16(nw2), op=ALU.mult))
            m_fw = sch.op("vector", lambda e: e.tensor_tensor(fw_sb[:], rb[:], rd[:], op=ALU.add), inc=SV)
            sg_busy = [xn_done, xn_done]

            if debug:
                sch.op("sync", lambda e, sbase=sbase: e.dma_start(
                    out=dbg_xn[sbase:sbase + S, :].rearrange("(tc p) d -> p tc d", p=128),
                    in_=xn_tm[:].rearrange("p (tc d) -> p tc d", tc=LNT)),
                    waits=[(SV, xn_done)], inc="SX_B", inc_amt=16)
                sch.op("sync", lambda e, seq=seq: e.dma_start(
                    out=dbg_fw[seq], in_=fw_sb[:]), waits=[(SV, m_fw)], inc="SX_B", inc_amt=16)
                sch.op("sync", lambda e, seq=seq: e.dma_start(
                    out=dbg_ls[seq], in_=dls[:]), inc="SX_B", inc_amt=16)
                sch.op("sync", lambda e, seq=seq: e.dma_start(
                    out=dbg_xf[seq], in_=xn_fm[:]), inc="SX_B", inc_amt=16)

            for tg in range(NTG):
                tgg = seq * NTG + tg
                tbase = sbase + tg * TG
                m_ai = None
                for tc in range(NTC):
                    w_ai = [(SO, store_mark.get(tgg - 1))] if tc == 0 else []
                    m_ai = sch.op("sync", lambda e, tc=tc, tbase=tbase: e.dma_start(
                        out=acc[:, tc * D:(tc + 1) * D],
                        in_=x_h[tbase + tc * 128: tbase + (tc + 1) * 128, :]),
                        waits=w_ai, inc="SX_A", inc_amt=16)

                for qc in range(NTC):
                    qt = tg * NTC + qc
                    pp = qc % 2
                    m_xw = None
                    for dc in range(NDC):
                        m_xw = sch.op("vector", lambda e, dc=dc, pp=pp, qt=qt: e.tensor_scalar(
                            xw_buf[pp][:, dc * 128:(dc + 1) * 128],
                            xn_fm[:, dc * S + qt * 128: dc * S + (qt + 1) * 128],
                            wlr[:, dc:dc + 1], 0.0, op0=ALU.mult, op1=ALU.add),
                            waits=[(SV, m_wlr), (SV, xn_done)],
                            inc=(SV if dc == NDC - 1 else None))
                    m_sc = None
                    for dc in range(NDC):
                        for kt in range(4):
                            m_sc = sch.op("tensor", lambda e, dc=dc, kt=kt, pp=pp: e.matmul(
                                P[kt][:],
                                xw_buf[pp][:, dc * 128:(dc + 1) * 128],
                                xn_fm[:, dc * S + kt * 512: dc * S + (kt + 1) * 512],
                                start=(dc == 0), stop=(dc == NDC - 1)),
                                waits=([(SV, m_xw)] + (bwait("P0", "P1", "P2", "P3") if dc == 0 else [])),
                                inc=(ST if (dc == NDC - 1 and kt == 3) else None))
                    m_m4 = None
                    for kt in range(4):
                        m_m4 = sch.op("vector", lambda e, kt=kt: e.reduce_max(
                            astat[:, kt:kt + 1], P[kt][:], axis=AX.X), waits=[(ST, m_sc)],
                            inc=(SV if kt == 3 else None))
                    m_m4b = sch.op("scalar", lambda e: e.copy(astat[:, 4:8], astat[:, 0:4]),
                                   waits=[(SV, m_m4)], inc=SS)
                    m_nm = sch.op("vector", lambda e: e.reduce_max(
                        astat[:, 8:9], astat[:, 4:8], axis=AX.X, negate=True),
                        waits=[(SS, m_m4b)], inc=SV)
                    m_exp = None
                    for kt in range(4):
                        m_exp = sch.op("scalar", lambda e, kt=kt: e.activation(
                            attn_sb[:, kt * 512:(kt + 1) * 512], P[kt][:], AF.Exp,
                            bias=astat[:, 8:9], accum_out=astat[:, 9 + kt:10 + kt]),
                            waits=[(SV, m_nm)], inc=(SS if kt == 3 else None))
                    for i in range(4):
                        bankfree[f"P{i}"] = (SS, m_exp)
                    m_su = sch.op("vector", lambda e: e.reduce_sum(
                        astat[:, 13:14], astat[:, 9:13], axis=AX.X), waits=[(SS, m_exp)], inc=SV)
                    m_sb = sch.op("scalar", lambda e: e.copy(astat[:, 14:15], astat[:, 13:14]),
                                  waits=[(SV, m_su)], inc=SS)
                    m_rc2 = sch.op("vector", lambda e: e.reciprocal(astat[:, 15:16], astat[:, 14:15]),
                                   waits=[(SS, m_sb)], inc=SV)
                    m_rec = sch.op("scalar", lambda e: e.copy(astat[:, 16:17], astat[:, 15:16]),
                                   waits=[(SV, m_rc2)], inc=SS)
                    m_tr = [None, None]
                    for half in range(2):
                        for j in range(8):
                            kc = half * 8 + j
                            m_tr[half] = sch.op("tensor", lambda e, half=half, j=j, kc=kc: e.transpose(
                                T[half][:, j * 128:(j + 1) * 128],
                                attn_sb[:, kc * 128:(kc + 1) * 128], ident[:]),
                                waits=([(SS, m_exp)] + (bwait(f"T{half}") if j == 0 else [])),
                                inc=(ST if j == 7 else None))
                    m_te = [None, None]
                    for half in range(2):
                        m_te[half] = sch.op("vector", lambda e, half=half: e.tensor_copy(
                            attnT_sb[:, half * 1024:(half + 1) * 1024], T[half][:]),
                            waits=[(ST, m_tr[half])], inc=SV)
                        bankfree[f"T{half}"] = (SV, m_te[half])
                    m_av = None
                    for kc in range(16):
                        for dh in range(2):
                            m_av = sch.op("tensor", lambda e, kc=kc, dh=dh: e.matmul(
                                P[4 + dh][:],
                                attnT_sb[:, kc * 128:(kc + 1) * 128],
                                xn_tm[:, kc * D + dh * 512: kc * D + (dh + 1) * 512],
                                start=(kc == 0), stop=(kc == 15)),
                                waits=([(SV, m_te[kc // 8])] + (bwait("P4", "P5") if kc == 0 else [])),
                                inc=(ST if (kc == 15 and dh == 1) else None))
                    m_ae = None
                    for dh in range(2):
                        m_ae = sch.op("vector", lambda e, dh=dh, qc=qc: e.scalar_tensor_tensor(
                            acc[:, qc * D + dh * 512: qc * D + (dh + 1) * 512],
                            P[4 + dh][:], astat[:, 16:17],
                            acc[:, qc * D + dh * 512: qc * D + (dh + 1) * 512],
                            op0=ALU.mult, op1=ALU.add),
                            waits=[(ST, m_av), (SS, m_rec), ("SX_A", m_ai)], inc=SV)
                    bankfree["P4"] = (SV, m_ae)
                    bankfree["P5"] = (SV, m_ae)

                for e_i in range(NE):
                    m_wg = {}
                    m_wu = {}

                    def load_wg(fc):
                        pp = fc % 2
                        m_wg[fc] = sch.op("sync", lambda e, e_i=e_i, fc=fc, pp=pp: e.dma_start(
                            out=wg_buf[pp][:], in_=wgc[e_i, fc]),
                            waits=[build_done[(e_i, "g")], (ST, wg_busy[pp])],
                            inc=f"SX_G{pp}", inc_amt=16)

                    def load_wu(fc):
                        pp = fc % 2
                        m_wu[fc] = sch.op("sync", lambda e, e_i=e_i, fc=fc, pp=pp: e.dma_start(
                            out=wu_buf[pp][:], in_=wuc[e_i, fc]),
                            waits=[build_done[(e_i, "u")], (ST, wu_busy[pp])],
                            inc=f"SX_U{pp}", inc_amt=16)

                    for fc in (0, 1):
                        load_wg(fc)
                        load_wu(fc)
                    m_wd = None
                    for fc in range(NFC):
                        m_wd = sch.op("sync", lambda e, e_i=e_i, fc=fc: e.dma_start(
                            out=wd_all[:, fc * D:(fc + 1) * D], in_=wdc[e_i, fc]),
                            waits=[build_done[(e_i, "d")]] + ([(ST, wd_free)] if fc == 0 else []),
                            inc="SX_D", inc_amt=16)

                    m_hm_last = None
                    for fc in range(NFC):
                        pp = fc % 2
                        gpair = (0, 1)
                        upair = (2, 3)
                        m_g = None
                        for dc in range(NDC):
                            for tt in range(2):
                                m_g = sch.op("tensor", lambda e, dc=dc, tt=tt, pp=pp, gpair=gpair, tg=tg: e.matmul(
                                    P[gpair[tt]][:],
                                    wg_buf[pp][:, dc * 128:(dc + 1) * 128],
                                    xn_fm[:, dc * S + tg * TG + tt * 512: dc * S + tg * TG + (tt + 1) * 512],
                                    start=(dc == 0), stop=(dc == NDC - 1)),
                                    waits=([(f"SX_G{pp}", m_wg[fc])] + (bwait(f"P{gpair[0]}", f"P{gpair[1]}") if dc == 0 else [])),
                                    inc=(ST if (dc == NDC - 1 and tt == 1) else None))
                        wg_busy[pp] = m_g
                        if fc + 2 < NFC:
                            load_wg(fc + 2)
                        m_si = None
                        for tt in range(2):
                            m_si = sch.op("scalar", lambda e, tt=tt, pp=pp, gpair=gpair: e.activation(
                                sg_t[pp][:, tt * 512:(tt + 1) * 512], P[gpair[tt]][:], AF.Silu),
                                waits=[(ST, m_g), (SV, sg_busy[pp])],
                                inc=(SS if tt == 1 else None))
                        bankfree[f"P{gpair[0]}"] = (SS, m_si)
                        bankfree[f"P{gpair[1]}"] = (SS, m_si)
                        m_u = None
                        for dc in range(NDC):
                            for tt in range(2):
                                m_u = sch.op("tensor", lambda e, dc=dc, tt=tt, pp=pp, upair=upair, tg=tg: e.matmul(
                                    P[upair[tt]][:],
                                    wu_buf[pp][:, dc * 128:(dc + 1) * 128],
                                    xn_fm[:, dc * S + tg * TG + tt * 512: dc * S + tg * TG + (tt + 1) * 512],
                                    start=(dc == 0), stop=(dc == NDC - 1)),
                                    waits=([(f"SX_U{pp}", m_wu[fc])] + (bwait(f"P{upair[0]}", f"P{upair[1]}") if dc == 0 else [])),
                                    inc=(ST if (dc == NDC - 1 and tt == 1) else None))
                        wu_busy[pp] = m_u
                        if fc + 2 < NFC:
                            load_wu(fc + 2)
                        m_hm = None
                        w_h = [(ST, h_guard)] if (h_guard is not None and fc == 0) else []
                        for tt in range(2):
                            m_hm = sch.op("vector", lambda e, tt=tt, pp=pp, fc=fc, upair=upair: e.tensor_tensor(
                                h_sb[:, fc * TG + tt * 512: fc * TG + (tt + 1) * 512],
                                sg_t[pp][:, tt * 512:(tt + 1) * 512], P[upair[tt]][:], op=ALU.mult),
                                waits=[(SS, m_si), (ST, m_u)] + (w_h if tt == 0 else []),
                                inc=(SV if tt == 1 else None))
                        sg_busy[pp] = m_hm
                        bankfree[f"P{upair[0]}"] = (SV, m_hm)
                        bankfree[f"P{upair[1]}"] = (SV, m_hm)
                        m_hm_last = m_hm
                    last_hmult[seq] = m_hm_last

                    m_dn_last = None
                    m_de = None
                    for tc in range(NTC):
                        dpair = (4, 5) if tc % 2 == 0 else (2, 3)
                        m_d = None
                        for fc in range(NFC):
                            for ddh in range(2):
                                m_d = sch.op("tensor", lambda e, fc=fc, ddh=ddh, tc=tc, dpair=dpair: e.matmul(
                                    P[dpair[ddh]][:],
                                    h_sb[:, fc * TG + tc * 128: fc * TG + (tc + 1) * 128],
                                    wd_all[:, fc * D + ddh * 512: fc * D + (ddh + 1) * 512],
                                    start=(fc == 0), stop=(fc == NFC - 1)),
                                    waits=([(SV, m_hm_last), ("SX_D", m_wd)] + (bwait(f"P{dpair[0]}", f"P{dpair[1]}") if fc == 0 else [])),
                                    inc=(ST if (fc == NFC - 1 and ddh == 1) else None))
                        m_dn_last = m_d
                        fcol = (tg * NTC + tc) * NE + e_i
                        for ddh in range(2):
                            m_de = sch.op("vector", lambda e, ddh=ddh, tc=tc, dpair=dpair, fcol=fcol: e.scalar_tensor_tensor(
                                acc[:, tc * D + ddh * 512: tc * D + (ddh + 1) * 512],
                                P[dpair[ddh]][:], fw_sb[:, fcol:fcol + 1],
                                acc[:, tc * D + ddh * 512: tc * D + (ddh + 1) * 512],
                                op0=ALU.mult, op1=ALU.add),
                                waits=[(ST, m_d), (SV, m_fw)], inc=SV)
                        bankfree[f"P{dpair[0]}"] = (SV, m_de)
                        bankfree[f"P{dpair[1]}"] = (SV, m_de)
                    h_guard = m_dn_last
                    wd_free = m_dn_last
                    xn_guard = m_dn_last
                    acc_done = m_de

                m_st = None
                for tc in range(NTC):
                    m_st = sch.op("sync", lambda e, tc=tc, tbase=tbase: e.dma_start(
                        out=out_h[tbase + tc * 128: tbase + (tc + 1) * 128, :],
                        in_=acc[:, tc * D:(tc + 1) * D]),
                        waits=[(SV, acc_done)], inc=SO, inc_amt=16)
                store_mark[tgg] = m_st

        total_so = sch.cnt[SO]
        sch.op("sync", lambda e: e.nop(), waits=[(SO, total_so)])

        with nc.Block() as block:
            for eng_name in ENGINES:
                ops = sch.ops[eng_name]

                def make(ops):
                    def emitter(eng):
                        waited = {}
                        for fn, waits, inc, inc_amt in ops:
                            for s, v in waits:
                                if waited.get(s, 0) < v:
                                    eng.wait_ge(sems[s], v)
                                    waited[s] = v
                            inst = fn(eng)
                            if inc is not None:
                                inst.then_inc(sems[inc], inc_amt)
                    return emitter

                getattr(block, eng_name)(make(ops))

    return nc


_NC_CACHE = {}


def _get_nc(debug=False):
    if debug not in _NC_CACHE:
        _NC_CACHE[debug] = build_nc(debug)
    return _NC_CACHE[debug]


def make_inputs(L0_attn_x, L0_attn_w_lhs, L0_attn_w_rhs, L0_ffn_router_kernel_DE,
                L0_ffn_kernel_gating_EDF, L0_ffn_kernel_up_EDF, L0_ffn_kernel_down_EFD):
    x = np.ascontiguousarray(np.asarray(L0_attn_x, dtype=np.float32))
    wl = np.ascontiguousarray(np.asarray(L0_attn_w_lhs, dtype=np.float32).reshape(NDC, 128).T)
    wr = np.ascontiguousarray(np.asarray(L0_attn_w_rhs, dtype=np.float32).reshape(NDC, 128).T)
    rt = np.ascontiguousarray(np.asarray(L0_ffn_router_kernel_DE, dtype=np.float32))
    wg = np.ascontiguousarray(np.asarray(L0_ffn_kernel_gating_EDF, dtype=np.float32))
    wu = np.ascontiguousarray(np.asarray(L0_ffn_kernel_up_EDF, dtype=np.float32))
    wd = np.ascontiguousarray(np.asarray(L0_ffn_kernel_down_EFD, dtype=np.float32))
    ident = np.eye(128, dtype=ml_dtypes.bfloat16)
    c8 = np.broadcast_to((NE - np.arange(NE, dtype=np.float32)), (128, NE)).copy()
    in_maps = []
    for c in range(NCORES):
        xs = x[c * SEQ_PER_CORE:(c + 1) * SEQ_PER_CORE].reshape(TOKENS, D)
        in_maps.append(dict(x=np.ascontiguousarray(xs), wl=wl, wr=wr, rt=rt,
                            wg=wg, wu=wu, wd=wd, ident=ident, c8=c8))
    return in_maps


def kernel(L0_attn_x, L0_attn_w_lhs, L0_attn_w_rhs, L0_ffn_router_kernel_DE,
           L0_ffn_kernel_gating_EDF, L0_ffn_kernel_up_EDF, L0_ffn_kernel_down_EFD):
    in_maps = make_inputs(L0_attn_x, L0_attn_w_lhs, L0_attn_w_rhs, L0_ffn_router_kernel_DE,
                          L0_ffn_kernel_gating_EDF, L0_ffn_kernel_up_EDF, L0_ffn_kernel_down_EFD)
    nc = _get_nc(debug=False)
    res = run_bass_kernel_spmd(nc, in_maps, core_ids=list(range(NCORES)))
    out = np.concatenate([np.asarray(r["out"], dtype=np.float32).reshape(SEQ_PER_CORE, S, D)
                          for r in res.results], axis=0)
    return out


# version 03
# speedup vs baseline: 1.1518x; 1.0451x over previous
import numpy as np
import ml_dtypes
import concourse.bass as bass
import concourse.mybir as mybir
from concourse.bass_utils import run_bass_kernel_spmd
from contextlib import ExitStack

F32 = mybir.dt.float32
BF16 = mybir.dt.bfloat16
AF = mybir.ActivationFunctionType
ALU = mybir.AluOpType
AX = mybir.AxisListType

B, S, D, FF, NE = 16, 2048, 1024, 2048, 8
NCORES = 8
SEQ_PER_CORE = B // NCORES
TOKENS = SEQ_PER_CORE * S
TG = 1024
NTG = S // TG
NDC = D // 128
NFC = FF // 128
NTC = TG // 128
LNT = S // 128
EPS = 1e-5

ENGINES = ("sync", "gpsimd", "vector", "scalar", "tensor")


class Sched:

    def __init__(self):
        self.ops = {e: [] for e in ENGINES}
        self.cnt = {}

    def op(self, eng, fn, waits=(), inc=None, inc_amt=1):
        mark = None
        if inc is not None:
            self.cnt[inc] = self.cnt.get(inc, 0) + inc_amt
            mark = self.cnt[inc]
        self.ops[eng].append((fn, tuple(w for w in waits if w[1] is not None), inc, inc_amt))
        return mark


def build_nc(debug=False):
    nc = bass.Bass("TRN2", target_bir_lowering=False)

    x_h = nc.dram_tensor("x", [TOKENS, D], F32, kind="ExternalInput").ap()
    wl_h = nc.dram_tensor("wl", [128, NDC], F32, kind="ExternalInput").ap()
    wr_h = nc.dram_tensor("wr", [128, NDC], F32, kind="ExternalInput").ap()
    rt_h = nc.dram_tensor("rt", [D, NE], F32, kind="ExternalInput").ap()
    wg_h = nc.dram_tensor("wg", [NE, D, FF], F32, kind="ExternalInput").ap()
    wu_h = nc.dram_tensor("wu", [NE, D, FF], F32, kind="ExternalInput").ap()
    wd_h = nc.dram_tensor("wd", [NE, FF, D], F32, kind="ExternalInput").ap()
    id_h = nc.dram_tensor("ident", [128, 128], BF16, kind="ExternalInput").ap()
    c8_h = nc.dram_tensor("c8", [128, NE], F32, kind="ExternalInput").ap()
    out_h = nc.dram_tensor("out", [TOKENS, D], F32, kind="ExternalOutput").ap()

    wgc = nc.dram_tensor("wgc", [NE, NFC, 128, D], BF16, kind="Internal").ap()
    wuc = nc.dram_tensor("wuc", [NE, NFC, 128, D], BF16, kind="Internal").ap()
    wdc = nc.dram_tensor("wdc", [NE, NFC, 128, D], BF16, kind="Internal").ap()

    if debug:
        dbg_xn = nc.dram_tensor("dbg_xn", [TOKENS, D], BF16, kind="ExternalOutput").ap()
        dbg_fw = nc.dram_tensor("dbg_fw", [SEQ_PER_CORE, 128, 128], F32, kind="ExternalOutput").ap()
        dbg_ls = nc.dram_tensor("dbg_ls", [SEQ_PER_CORE, 128, 64], F32, kind="ExternalOutput").ap()
        dbg_xf = nc.dram_tensor("dbg_xf", [SEQ_PER_CORE, 128, NDC * S], BF16, kind="ExternalOutput").ap()

    sch = Sched()
    SV, SS, ST, SO = "SV", "SS", "ST", "SO"
    SEM_NAMES = (["SX_C", "SX_X0", "SX_X1", "SX_A", "SX_D",
                  "SX_G0", "SX_G1", "SX_U0", "SX_U1", "SO", "SW_R", "SX_B",
                  "SV", "SS", "ST"]
                 + [f"SWb{e}{m}" for e in range(NE) for m in "gud"])

    with ExitStack() as ctx:
        E = ctx.enter_context
        xn_tm = E(nc.sbuf_tensor("xn_tm", [128, LNT * D], BF16))
        xn_fm = E(nc.sbuf_tensor("xn_fm", [128, NDC * S], BF16))
        acc = E(nc.sbuf_tensor("acc", [128, NTC * D], F32))
        h_sb = E(nc.sbuf_tensor("h_sb", [128, NFC * TG], BF16))
        wd_all = E(nc.sbuf_tensor("wd_all", [128, NFC * D], BF16))
        wg_buf = [E(nc.sbuf_tensor(f"wgb{i}", [128, D], BF16)) for i in range(2)]
        wu_buf = [E(nc.sbuf_tensor(f"wub{i}", [128, D], BF16)) for i in range(2)]
        unionA = E(nc.sbuf_tensor("unionA", [128, 3072], F32))
        sg_sb = E(nc.sbuf_tensor("sg_sb", [128, 2048], F32))
        xw_buf = [E(nc.sbuf_tensor(f"xwb{i}", [128, NDC * 128], BF16)) for i in range(2)]
        attn_sb = E(nc.sbuf_tensor("attn_sb", [128, S], BF16))
        attnT_sb = E(nc.sbuf_tensor("attnT_sb", [128, S], BF16))
        astat = E(nc.sbuf_tensor("astat", [128, 20], F32))
        lstat = E(nc.sbuf_tensor("lstat", [128, 160], F32))
        logits_sb = E(nc.sbuf_tensor("logits_sb", [128, 128], F32))
        fw_sb = E(nc.sbuf_tensor("fw_sb", [128, 128], F32))
        ra = E(nc.sbuf_tensor("ra", [128, 128], F32))
        rb = E(nc.sbuf_tensor("rb", [128, 128], F32))
        rc = E(nc.sbuf_tensor("rc", [128, 128], F32))
        rd = E(nc.sbuf_tensor("rd", [128, 128], F32))
        r16 = E(nc.sbuf_tensor("r16", [128, 288], F32))
        ident = E(nc.sbuf_tensor("ident_sb", [128, 128], BF16))
        c8 = E(nc.sbuf_tensor("c8_sb", [128, NE], F32))
        wl_sb = E(nc.sbuf_tensor("wl_sb", [128, NDC], F32))
        wr_sb = E(nc.sbuf_tensor("wr_sb", [128, NDC], F32))
        wlr = E(nc.sbuf_tensor("wlr", [128, NDC], F32))
        rt_bf = E(nc.sbuf_tensor("rt_bf", [128, NDC * NE], BF16))
        dls = E(nc.sbuf_tensor("dls", [128, 64], F32)) if debug else None

        x_t = [unionA[:, 0:1024], unionA[:, 1024:2048]]
        sqdump = unionA[:, 2048:3072]
        sg_t = [sg_sb[:, 0:1024], sg_sb[:, 1024:2048]]

        P = [E(nc.psum_tensor(f"P{i}", [128, 512], F32)) for i in range(6)]
        T = [E(nc.psum_tensor(f"T{i}", [128, 1024], BF16)) for i in range(2)]

        sems = {n: E(nc.semaphore(n)) for n in SEM_NAMES}

        bankfree = {f"P{i}": None for i in range(6)}
        bankfree.update({"T0": None, "T1": None})

        def bwait(*names):
            return [bankfree[n] for n in names if bankfree[n] is not None]

        wg_busy = [None, None]
        wu_busy = [None, None]
        sg_busy = [None, None]
        h_guard = None
        wd_free = None
        xn_guard = None
        store_mark = {}
        xn_last_prev = None

        sch.op("sync", lambda e: e.dma_start(out=ident[:], in_=id_h), inc="SX_C", inc_amt=16)
        sch.op("sync", lambda e: e.dma_start(out=c8[:], in_=c8_h), inc="SX_C", inc_amt=16)
        sch.op("sync", lambda e: e.dma_start(out=wl_sb[:], in_=wl_h), inc="SX_C", inc_amt=16)
        sch.op("sync", lambda e: e.dma_start(out=wr_sb[:], in_=wr_h), inc="SX_C", inc_amt=16)
        M_C = 64
        m_wlr = sch.op("vector", lambda e: e.tensor_tensor(wlr[:], wl_sb[:], wr_sb[:], op=ALU.mult),
                       waits=[("SX_C", M_C)], inc=SV)
        for dc in range(NDC):
            sch.op("gpsimd", lambda e, dc=dc: e.dma_start(
                out=rt_bf[:, dc * NE:(dc + 1) * NE], in_=rt_h[dc * 128:(dc + 1) * 128, :]),
                inc="SW_R", inc_amt=16)
        M_RT = 128

        build_done = {}
        for e_i in range(NE):
            sem = f"SWb{e_i}d"
            m = sch.op("gpsimd", lambda e, e_i=e_i: e.dma_start(
                out=wdc[e_i].rearrange("fc p dd -> (fc p) dd"), in_=wd_h[e_i]),
                inc=sem, inc_amt=16)
            build_done[(e_i, "d")] = (sem, m)
            for mat, src_h, dst_h in (("g", wg_h, wgc), ("u", wu_h, wuc)):
                sem = f"SWb{e_i}{mat}"
                m = None
                for dc in range(NDC):
                    for q in range(4):
                        fc0, fc1 = q * 4, (q + 1) * 4
                        m = sch.op("gpsimd", lambda e, e_i=e_i, dc=dc, fc0=fc0, fc1=fc1,
                                   src_h=src_h, dst_h=dst_h: e.dma_start(
                            out=dst_h[e_i, fc0:fc1, :, dc * 128:(dc + 1) * 128].rearrange("fc p f -> p fc f"),
                            in_=src_h[e_i, dc * 128:(dc + 1) * 128, fc0 * 128:fc1 * 128].rearrange("p (fc f) -> p fc f", f=128)),
                            inc=sem, inc_amt=16)
                build_done[(e_i, mat)] = (sem, m)

        for seq in range(SEQ_PER_CORE):
            sbase = seq * S
            rs16 = lstat[:, 0:16]
            ssq16 = lstat[:, 16:32]
            mean16 = lstat[:, 32:48]
            ssqm16 = lstat[:, 48:64]
            lm2 = lstat[:, 64:80]
            lm2b = lstat[:, 80:96]
            lvar = lstat[:, 96:112]
            lsd = lstat[:, 112:128]
            lri = lstat[:, 128:144]
            lri2 = lstat[:, 144:160]

            m_ssq = {}
            for tc in range(LNT):
                p = tc % 2
                w_ln = []
                if seq > 0 and tc < 2:
                    w_ln.append((SV, xn_last_prev))
                if tc >= 2:
                    w_ln.append((SV, m_ssq[tc - 2]))
                m_x = sch.op("sync", lambda e, tc=tc, p=p, sbase=sbase: e.dma_start(
                    out=x_t[p], in_=x_h[sbase + tc * 128: sbase + (tc + 1) * 128, :]),
                    waits=w_ln, inc=f"SX_X{p}", inc_amt=16)
                sch.op("vector", lambda e, tc=tc, p=p: e.reduce_sum(
                    rs16[:, tc:tc + 1], x_t[p], axis=AX.X), waits=[(f"SX_X{p}", m_x)])
                sch.op("vector", lambda e, p=p: e.tensor_tensor(
                    sqdump, x_t[p], x_t[p], op=ALU.mult))
                m_ssq[tc] = sch.op("vector", lambda e, tc=tc: e.reduce_sum(
                    ssq16[:, tc:tc + 1], sqdump, axis=AX.X), inc=SV)

            sch.op("scalar", lambda e: e.mul(mean16, rs16, 1.0 / D), waits=[(SV, m_ssq[LNT - 1])])
            m_l2 = sch.op("scalar", lambda e: e.mul(ssqm16, ssq16, 1.0 / D), inc=SS)
            m_m2 = sch.op("vector", lambda e: e.tensor_tensor(lm2, mean16, mean16, op=ALU.mult),
                          waits=[(SS, m_l2)], inc=SV)
            m_m2b = sch.op("scalar", lambda e: e.copy(lm2b, lm2), waits=[(SV, m_m2)], inc=SS)
            m_var = sch.op("vector", lambda e: e.scalar_tensor_tensor(
                lvar, ssqm16, 1.0, lm2b, op0=ALU.mult, op1=ALU.subtract),
                waits=[(SS, m_m2b)], inc=SV)
            m_sdm = sch.op("scalar", lambda e: e.activation(lsd, lvar, AF.Sqrt, bias=np.float32(EPS)),
                           waits=[(SV, m_var)], inc=SS)
            m_ri = sch.op("vector", lambda e: e.reciprocal(lri, lsd), waits=[(SS, m_sdm)], inc=SV)
            m_ri2 = sch.op("scalar", lambda e: e.copy(lri2, lri), waits=[(SV, m_ri)], inc=SS)
            if debug:
                sch.op("scalar", lambda e: e.copy(dls[:, 0:16], mean16))
                sch.op("scalar", lambda e: e.copy(dls[:, 32:48], lri2))

            xn_done = None
            m_xn = {}
            m_lc = None
            for tc in range(LNT):
                p = tc % 2
                w3 = [(SV, m_ssq[LNT - 2 + p])] if tc < 2 else [(SV, m_xn[tc - 2])]
                m_x = sch.op("sync", lambda e, tc=tc, p=p, sbase=sbase: e.dma_start(
                    out=x_t[p], in_=x_h[sbase + tc * 128: sbase + (tc + 1) * 128, :]),
                    waits=w3, inc=f"SX_X{p}", inc_amt=16)
                w_xn = [(f"SX_X{p}", m_x), (SS, m_ri2)]
                if seq > 0 and tc == 0:
                    w_xn.append((ST, xn_guard))
                if debug and seq > 0 and tc == 0:
                    w_xn.append(("SX_B", 64 * seq))
                m_xn[tc] = sch.op("vector", lambda e, p=p, tc=tc: e.tensor_scalar(
                    xn_tm[:, tc * D:(tc + 1) * D], x_t[p], mean16[:, tc:tc + 1], lri2[:, tc:tc + 1],
                    op0=ALU.subtract, op1=ALU.mult), waits=w_xn, inc=SV)
                tb = f"T{p}"
                m_tr = None
                for dc in range(NDC):
                    m_tr = sch.op("tensor", lambda e, p=p, tc=tc, dc=dc: e.transpose(
                        T[p][:, dc * 128:(dc + 1) * 128],
                        xn_tm[:, tc * D + dc * 128: tc * D + (dc + 1) * 128], ident[:]),
                        waits=([(SV, m_xn[tc]), ("SX_C", M_C)] + (bwait(tb) if dc == 0 else [])),
                        inc=(ST if dc == NDC - 1 else None))
                m_ev = sch.op("vector", lambda e, p=p, tc=tc: e.tensor_copy(
                    xn_fm[:].rearrange("p (dc t) -> p dc t", dc=NDC)[:, :, tc * 128:(tc + 1) * 128],
                    T[p][:].rearrange("p (dc t) -> p dc t", dc=NDC)),
                    waits=[(ST, m_tr)], inc=SV)
                bankfree[tb] = (SV, m_ev)
                xn_done = m_ev
                m_lg = None
                for dc in range(NDC):
                    m_lg = sch.op("tensor", lambda e, tc=tc, dc=dc: e.matmul(
                        P[4][:, 0:NE],
                        xn_fm[:, dc * S + tc * 128: dc * S + (tc + 1) * 128],
                        rt_bf[:, dc * NE:(dc + 1) * NE],
                        start=(dc == 0), stop=(dc == NDC - 1)),
                        waits=([(SV, m_ev), ("SW_R", M_RT)] + (bwait("P4") if dc == 0 else [])),
                        inc=(ST if dc == NDC - 1 else None))
                m_lc = sch.op("scalar", lambda e, tc=tc: e.copy(
                    logits_sb[:, tc * NE:(tc + 1) * NE], P[4][:, 0:NE]),
                    waits=[(ST, m_lg)], inc=SS)
                bankfree["P4"] = (SS, m_lc)
            xn_last_prev = m_xn[LNT - 1]

            def a3(t):
                return t[:].rearrange("p (t e) -> p t e", e=NE)

            def bc16(ap16):
                return ap16.unsqueeze(2).broadcast_to([128, 16, NE])

            def bc8(ap8):
                return ap8.unsqueeze(1).broadcast_to([128, 16, NE])

            nmax = r16[:, 0:16]
            nmaxb = r16[:, 16:32]
            rsum = r16[:, 32:48]
            rsumb = r16[:, 48:64]
            rrec = r16[:, 64:80]
            rrecb = r16[:, 80:96]
            m1 = r16[:, 96:112]
            m1b = r16[:, 112:128]
            m2 = r16[:, 128:144]
            m2b = r16[:, 144:160]
            c1 = r16[:, 160:176]
            c1b = r16[:, 176:192]
            c2 = r16[:, 192:208]
            c2b = r16[:, 208:224]
            d21 = r16[:, 224:240]
            nw1 = r16[:, 240:256]
            nw2 = r16[:, 256:272]
            mv = sch.op("vector", lambda e: e.tensor_reduce(nmax, a3(logits_sb), axis=AX.X, op=ALU.max, negate=True),
                        waits=[(SS, m_lc)], inc=SV)
            ms = sch.op("scalar", lambda e: e.copy(nmaxb, nmax), waits=[(SV, mv)], inc=SS)
            mv = sch.op("vector", lambda e: e.tensor_tensor(a3(rb), a3(logits_sb), bc16(nmaxb), op=ALU.add),
                        waits=[(SS, ms)], inc=SV)
            ms = sch.op("scalar", lambda e: e.activation(ra[:], rb[:], AF.Exp), waits=[(SV, mv)], inc=SS)
            mv = sch.op("vector", lambda e: e.tensor_reduce(rsum, a3(ra), axis=AX.X, op=ALU.add),
                        waits=[(SS, ms)], inc=SV)
            ms = sch.op("scalar", lambda e: e.copy(rsumb, rsum), waits=[(SV, mv)], inc=SS)
            mv = sch.op("vector", lambda e: e.reciprocal(rrec, rsumb), waits=[(SS, ms)], inc=SV)
            ms = sch.op("scalar", lambda e: e.copy(rrecb, rrec), waits=[(SV, mv)], inc=SS)
            sch.op("vector", lambda e: e.tensor_tensor(a3(rb), a3(ra), bc16(rrecb), op=ALU.mult),
                   waits=[(SS, ms)])
            mv = sch.op("vector", lambda e: e.tensor_reduce(m1, a3(rb), axis=AX.X, op=ALU.max), inc=SV)
            ms = sch.op("scalar", lambda e: e.copy(m1b, m1), waits=[(SV, mv)], inc=SS)
            sch.op("vector", lambda e: e.tensor_tensor(a3(rc), a3(rb), bc16(m1b), op=ALU.is_ge),
                   waits=[(SS, ms)])
            sch.op("vector", lambda e: e.tensor_tensor(a3(rd), a3(rc), bc8(c8[:]), op=ALU.mult))
            mv = sch.op("vector", lambda e: e.tensor_reduce(c1, a3(rd), axis=AX.X, op=ALU.max), inc=SV)
            ms = sch.op("scalar", lambda e: e.copy(c1b, c1), waits=[(SV, mv)], inc=SS)
            sch.op("vector", lambda e: e.tensor_tensor(a3(rc), a3(rd), bc16(c1b), op=ALU.is_ge),
                   waits=[(SS, ms)])
            sch.op("vector", lambda e: e.scalar_tensor_tensor(a3(rd), a3(rc), -2.0, a3(rb), op0=ALU.mult, op1=ALU.add))
            mv = sch.op("vector", lambda e: e.tensor_reduce(m2, a3(rd), axis=AX.X, op=ALU.max), inc=SV)
            ms = sch.op("scalar", lambda e: e.copy(m2b, m2), waits=[(SV, mv)], inc=SS)
            sch.op("vector", lambda e: e.tensor_tensor(a3(ra), a3(rd), bc16(m2b), op=ALU.is_ge),
                   waits=[(SS, ms)])
            sch.op("vector", lambda e: e.tensor_tensor(a3(rd), a3(ra), bc8(c8[:]), op=ALU.mult))
            mv = sch.op("vector", lambda e: e.tensor_reduce(c2, a3(rd), axis=AX.X, op=ALU.max), inc=SV)
            ms = sch.op("scalar", lambda e: e.copy(c2b, c2), waits=[(SV, mv)], inc=SS)
            sch.op("vector", lambda e: e.tensor_tensor(a3(ra), a3(rd), bc16(c2b), op=ALU.is_ge),
                   waits=[(SS, ms)])
            mv = sch.op("vector", lambda e: e.tensor_tensor(d21, m2b, m1b, op=ALU.subtract), inc=SV)
            sch.op("scalar", lambda e: e.activation(nw2, d21, AF.Sigmoid), waits=[(SV, mv)])
            ms = sch.op("scalar", lambda e: e.activation(nw1, d21, AF.Sigmoid, scale=np.float32(-1.0)), inc=SS)
            sch.op("vector", lambda e: e.tensor_tensor(a3(rb), a3(rc), bc16(nw1), op=ALU.mult),
                   waits=[(SS, ms)])
            sch.op("vector", lambda e: e.tensor_tensor(a3(rd), a3(ra), bc16(nw2), op=ALU.mult))
            m_fw = sch.op("vector", lambda e: e.tensor_tensor(fw_sb[:], rb[:], rd[:], op=ALU.add), inc=SV)
            sg_busy = [xn_done, xn_done]

            if debug:
                sch.op("sync", lambda e, sbase=sbase: e.dma_start(
                    out=dbg_xn[sbase:sbase + S, :].rearrange("(tc p) d -> p tc d", p=128),
                    in_=xn_tm[:].rearrange("p (tc d) -> p tc d", tc=LNT)),
                    waits=[(SV, xn_done)], inc="SX_B", inc_amt=16)
                sch.op("sync", lambda e, seq=seq: e.dma_start(
                    out=dbg_fw[seq], in_=fw_sb[:]), waits=[(SV, m_fw)], inc="SX_B", inc_amt=16)
                sch.op("sync", lambda e, seq=seq: e.dma_start(
                    out=dbg_ls[seq], in_=dls[:]), inc="SX_B", inc_amt=16)
                sch.op("sync", lambda e, seq=seq: e.dma_start(
                    out=dbg_xf[seq], in_=xn_fm[:]), inc="SX_B", inc_amt=16)

            for tg in range(NTG):
                tgg = seq * NTG + tg
                tbase = sbase + tg * TG
                m_ai = None
                for tc in range(NTC):
                    w_ai = [(SO, store_mark.get(tgg - 1))] if tc == 0 else []
                    m_ai = sch.op("sync", lambda e, tc=tc, tbase=tbase: e.dma_start(
                        out=acc[:, tc * D:(tc + 1) * D],
                        in_=x_h[tbase + tc * 128: tbase + (tc + 1) * 128, :]),
                        waits=w_ai, inc="SX_A", inc_amt=16)

                at = {}
                for qc in range(NTC + 1):
                    if qc < NTC:
                        qt = tg * NTC + qc
                        pp = qc % 2
                        m_xw = None
                        for dc in range(NDC):
                            m_xw = sch.op("vector", lambda e, dc=dc, pp=pp, qt=qt: e.tensor_scalar(
                                xw_buf[pp][:, dc * 128:(dc + 1) * 128],
                                xn_fm[:, dc * S + qt * 128: dc * S + (qt + 1) * 128],
                                wlr[:, dc:dc + 1], 0.0, op0=ALU.mult, op1=ALU.add),
                                waits=[(SV, m_wlr), (SV, xn_done)],
                                inc=(SV if dc == NDC - 1 else None))
                        m_sc = None
                        for dc in range(NDC):
                            for kt in range(4):
                                m_sc = sch.op("tensor", lambda e, dc=dc, kt=kt, pp=pp: e.matmul(
                                    P[kt][:],
                                    xw_buf[pp][:, dc * 128:(dc + 1) * 128],
                                    xn_fm[:, dc * S + kt * 512: dc * S + (kt + 1) * 512],
                                    start=(dc == 0), stop=(dc == NDC - 1)),
                                    waits=([(SV, m_xw)] + (bwait("P0", "P1", "P2", "P3") if dc == 0 else [])),
                                    inc=(ST if (dc == NDC - 1 and kt == 3) else None))
                        at[(qc, "sc")] = m_sc
                    if qc >= 1:
                        q1 = qc - 1
                        m_tr = [None, None]
                        for half in range(2):
                            for j in range(8):
                                kc = half * 8 + j
                                m_tr[half] = sch.op("tensor", lambda e, half=half, j=j, kc=kc: e.transpose(
                                    T[half][:, j * 128:(j + 1) * 128],
                                    attn_sb[:, kc * 128:(kc + 1) * 128], ident[:]),
                                    waits=([(SS, at[(q1, "exp")])] + (bwait(f"T{half}") if j == 0 else [])),
                                    inc=(ST if j == 7 else None))
                        at[(q1, "tr")] = m_tr
                    if qc < NTC:
                        m_m4 = None
                        for kt in range(4):
                            m_m4 = sch.op("vector", lambda e, kt=kt: e.reduce_max(
                                astat[:, kt:kt + 1], P[kt][:], axis=AX.X), waits=[(ST, at[(qc, "sc")])],
                                inc=(SV if kt == 3 else None))
                        m_m4b = sch.op("scalar", lambda e: e.copy(astat[:, 4:8], astat[:, 0:4]),
                                       waits=[(SV, m_m4)], inc=SS)
                        at[(qc, "nm")] = sch.op("vector", lambda e: e.reduce_max(
                            astat[:, 8:9], astat[:, 4:8], axis=AX.X, negate=True),
                            waits=[(SS, m_m4b)], inc=SV)
                    if qc >= 1:
                        q1 = qc - 1
                        m_tr = at[(q1, "tr")]
                        m_te = [None, None]
                        for half in range(2):
                            m_te[half] = sch.op("vector", lambda e, half=half: e.tensor_copy(
                                attnT_sb[:, half * 1024:(half + 1) * 1024], T[half][:]),
                                waits=[(ST, m_tr[half])], inc=SV)
                            bankfree[f"T{half}"] = (SV, m_te[half])
                        m_av = None
                        for kc in range(16):
                            for dh in range(2):
                                m_av = sch.op("tensor", lambda e, kc=kc, dh=dh: e.matmul(
                                    P[4 + dh][:],
                                    attnT_sb[:, kc * 128:(kc + 1) * 128],
                                    xn_tm[:, kc * D + dh * 512: kc * D + (dh + 1) * 512],
                                    start=(kc == 0), stop=(kc == 15)),
                                    waits=([(SV, m_te[kc // 8])] + (bwait("P4", "P5") if kc == 0 else [])),
                                    inc=(ST if (kc == 15 and dh == 1) else None))
                        at[(q1, "av")] = m_av
                    if qc < NTC:
                        m_exp = None
                        w_exp0 = [(ST, at[(qc - 1, "tr")][1])] if qc >= 1 else []
                        for kt in range(4):
                            m_exp = sch.op("scalar", lambda e, kt=kt: e.activation(
                                attn_sb[:, kt * 512:(kt + 1) * 512], P[kt][:], AF.Exp,
                                bias=astat[:, 8:9], accum_out=astat[:, 9 + kt:10 + kt]),
                                waits=[(SV, at[(qc, "nm")])] + (w_exp0 if kt == 0 else []),
                                inc=(SS if kt == 3 else None))
                        at[(qc, "exp")] = m_exp
                        for i in range(4):
                            bankfree[f"P{i}"] = (SS, m_exp)
                        rp = qc % 2
                        m_su = sch.op("vector", lambda e: e.reduce_sum(
                            astat[:, 13:14], astat[:, 9:13], axis=AX.X), waits=[(SS, m_exp)], inc=SV)
                        m_sb = sch.op("scalar", lambda e: e.copy(astat[:, 14:15], astat[:, 13:14]),
                                      waits=[(SV, m_su)], inc=SS)
                        m_rc2 = sch.op("vector", lambda e, rp=rp: e.reciprocal(
                            astat[:, 15 + 2 * rp:16 + 2 * rp], astat[:, 14:15]),
                            waits=[(SS, m_sb)], inc=SV)
                        at[(qc, "recb")] = sch.op("scalar", lambda e, rp=rp: e.copy(
                            astat[:, 16 + 2 * rp:17 + 2 * rp], astat[:, 15 + 2 * rp:16 + 2 * rp]),
                            waits=[(SV, m_rc2)], inc=SS)
                    if qc >= 1:
                        q1 = qc - 1
                        rp = q1 % 2
                        m_ae = None
                        for dh in range(2):
                            m_ae = sch.op("vector", lambda e, dh=dh, q1=q1, rp=rp: e.scalar_tensor_tensor(
                                acc[:, q1 * D + dh * 512: q1 * D + (dh + 1) * 512],
                                P[4 + dh][:], astat[:, 16 + 2 * rp:17 + 2 * rp],
                                acc[:, q1 * D + dh * 512: q1 * D + (dh + 1) * 512],
                                op0=ALU.mult, op1=ALU.add),
                                waits=[(ST, at[(q1, "av")]), (SS, at[(q1, "recb")]), ("SX_A", m_ai)], inc=SV)
                        bankfree["P4"] = (SV, m_ae)
                        bankfree["P5"] = (SV, m_ae)

                for e_i in range(NE):
                    m_wg = {}
                    m_wu = {}

                    def load_wg(fc):
                        pp = fc % 2
                        m_wg[fc] = sch.op("sync", lambda e, e_i=e_i, fc=fc, pp=pp: e.dma_start(
                            out=wg_buf[pp][:], in_=wgc[e_i, fc]),
                            waits=[build_done[(e_i, "g")], (ST, wg_busy[pp])],
                            inc=f"SX_G{pp}", inc_amt=16)

                    def load_wu(fc):
                        pp = fc % 2
                        m_wu[fc] = sch.op("sync", lambda e, e_i=e_i, fc=fc, pp=pp: e.dma_start(
                            out=wu_buf[pp][:], in_=wuc[e_i, fc]),
                            waits=[build_done[(e_i, "u")], (ST, wu_busy[pp])],
                            inc=f"SX_U{pp}", inc_amt=16)

                    for fc in (0, 1):
                        load_wg(fc)
                        load_wu(fc)
                    m_wd = None
                    for fc in range(NFC):
                        m_wd = sch.op("sync", lambda e, e_i=e_i, fc=fc: e.dma_start(
                            out=wd_all[:, fc * D:(fc + 1) * D], in_=wdc[e_i, fc]),
                            waits=[build_done[(e_i, "d")]] + ([(ST, wd_free)] if fc == 0 else []),
                            inc="SX_D", inc_amt=16)

                    m_hm_last = None
                    for fc in range(NFC):
                        pp = fc % 2
                        gpair = (0, 1)
                        upair = (2, 3)
                        m_g = None
                        for dc in range(NDC):
                            for tt in range(2):
                                m_g = sch.op("tensor", lambda e, dc=dc, tt=tt, pp=pp, gpair=gpair, tg=tg: e.matmul(
                                    P[gpair[tt]][:],
                                    wg_buf[pp][:, dc * 128:(dc + 1) * 128],
                                    xn_fm[:, dc * S + tg * TG + tt * 512: dc * S + tg * TG + (tt + 1) * 512],
                                    start=(dc == 0), stop=(dc == NDC - 1)),
                                    waits=([(f"SX_G{pp}", m_wg[fc])] + (bwait(f"P{gpair[0]}", f"P{gpair[1]}") if dc == 0 else [])),
                                    inc=(ST if (dc == NDC - 1 and tt == 1) else None))
                        wg_busy[pp] = m_g
                        if fc + 2 < NFC:
                            load_wg(fc + 2)
                        m_si = None
                        for tt in range(2):
                            m_si = sch.op("scalar", lambda e, tt=tt, pp=pp, gpair=gpair: e.activation(
                                sg_t[pp][:, tt * 512:(tt + 1) * 512], P[gpair[tt]][:], AF.Silu),
                                waits=[(ST, m_g), (SV, sg_busy[pp])],
                                inc=(SS if tt == 1 else None))
                        bankfree[f"P{gpair[0]}"] = (SS, m_si)
                        bankfree[f"P{gpair[1]}"] = (SS, m_si)
                        m_u = None
                        for dc in range(NDC):
                            for tt in range(2):
                                m_u = sch.op("tensor", lambda e, dc=dc, tt=tt, pp=pp, upair=upair, tg=tg: e.matmul(
                                    P[upair[tt]][:],
                                    wu_buf[pp][:, dc * 128:(dc + 1) * 128],
                                    xn_fm[:, dc * S + tg * TG + tt * 512: dc * S + tg * TG + (tt + 1) * 512],
                                    start=(dc == 0), stop=(dc == NDC - 1)),
                                    waits=([(f"SX_U{pp}", m_wu[fc])] + (bwait(f"P{upair[0]}", f"P{upair[1]}") if dc == 0 else [])),
                                    inc=(ST if (dc == NDC - 1 and tt == 1) else None))
                        wu_busy[pp] = m_u
                        if fc + 2 < NFC:
                            load_wu(fc + 2)
                        m_hm = None
                        w_h = [(ST, h_guard)] if (h_guard is not None and fc == 0) else []
                        for tt in range(2):
                            m_hm = sch.op("vector", lambda e, tt=tt, pp=pp, fc=fc, upair=upair: e.tensor_tensor(
                                h_sb[:, fc * TG + tt * 512: fc * TG + (tt + 1) * 512],
                                sg_t[pp][:, tt * 512:(tt + 1) * 512], P[upair[tt]][:], op=ALU.mult),
                                waits=[(SS, m_si), (ST, m_u)] + (w_h if tt == 0 else []),
                                inc=(SV if tt == 1 else None))
                        sg_busy[pp] = m_hm
                        bankfree[f"P{upair[0]}"] = (SV, m_hm)
                        bankfree[f"P{upair[1]}"] = (SV, m_hm)
                        m_hm_last = m_hm

                    m_dn_last = None
                    m_de = None
                    for tc in range(NTC):
                        dpair = (4, 5) if tc % 2 == 0 else (2, 3)
                        m_d = None
                        for fc in range(NFC):
                            for ddh in range(2):
                                m_d = sch.op("tensor", lambda e, fc=fc, ddh=ddh, tc=tc, dpair=dpair: e.matmul(
                                    P[dpair[ddh]][:],
                                    h_sb[:, fc * TG + tc * 128: fc * TG + (tc + 1) * 128],
                                    wd_all[:, fc * D + ddh * 512: fc * D + (ddh + 1) * 512],
                                    start=(fc == 0), stop=(fc == NFC - 1)),
                                    waits=([(SV, m_hm_last), ("SX_D", m_wd)] + (bwait(f"P{dpair[0]}", f"P{dpair[1]}") if fc == 0 else [])),
                                    inc=(ST if (fc == NFC - 1 and ddh == 1) else None))
                        m_dn_last = m_d
                        fcol = (tg * NTC + tc) * NE + e_i
                        for ddh in range(2):
                            m_de = sch.op("vector", lambda e, ddh=ddh, tc=tc, dpair=dpair, fcol=fcol: e.scalar_tensor_tensor(
                                acc[:, tc * D + ddh * 512: tc * D + (ddh + 1) * 512],
                                P[dpair[ddh]][:], fw_sb[:, fcol:fcol + 1],
                                acc[:, tc * D + ddh * 512: tc * D + (ddh + 1) * 512],
                                op0=ALU.mult, op1=ALU.add),
                                waits=[(ST, m_d), (SV, m_fw)], inc=SV)
                        bankfree[f"P{dpair[0]}"] = (SV, m_de)
                        bankfree[f"P{dpair[1]}"] = (SV, m_de)
                    h_guard = m_dn_last
                    wd_free = m_dn_last
                    xn_guard = m_dn_last
                    acc_done = m_de

                m_st = None
                for tc in range(NTC):
                    m_st = sch.op("sync", lambda e, tc=tc, tbase=tbase: e.dma_start(
                        out=out_h[tbase + tc * 128: tbase + (tc + 1) * 128, :],
                        in_=acc[:, tc * D:(tc + 1) * D]),
                        waits=[(SV, acc_done)], inc=SO, inc_amt=16)
                store_mark[tgg] = m_st

        total_so = sch.cnt[SO]
        sch.op("sync", lambda e: e.nop(), waits=[(SO, total_so)])

        with nc.Block() as block:
            for eng_name in ENGINES:
                ops = sch.ops[eng_name]

                def make(ops):
                    def emitter(eng):
                        waited = {}
                        for fn, waits, inc, inc_amt in ops:
                            for s, v in waits:
                                if waited.get(s, 0) < v:
                                    eng.wait_ge(sems[s], v)
                                    waited[s] = v
                            inst = fn(eng)
                            if inc is not None:
                                inst.then_inc(sems[inc], inc_amt)
                    return emitter

                getattr(block, eng_name)(make(ops))

    return nc


_NC_CACHE = {}


def _get_nc(debug=False):
    if debug not in _NC_CACHE:
        _NC_CACHE[debug] = build_nc(debug)
    return _NC_CACHE[debug]


def make_inputs(L0_attn_x, L0_attn_w_lhs, L0_attn_w_rhs, L0_ffn_router_kernel_DE,
                L0_ffn_kernel_gating_EDF, L0_ffn_kernel_up_EDF, L0_ffn_kernel_down_EFD):
    x = np.ascontiguousarray(np.asarray(L0_attn_x, dtype=np.float32))
    wl = np.ascontiguousarray(np.asarray(L0_attn_w_lhs, dtype=np.float32).reshape(NDC, 128).T)
    wr = np.ascontiguousarray(np.asarray(L0_attn_w_rhs, dtype=np.float32).reshape(NDC, 128).T)
    rt = np.ascontiguousarray(np.asarray(L0_ffn_router_kernel_DE, dtype=np.float32))
    wg = np.ascontiguousarray(np.asarray(L0_ffn_kernel_gating_EDF, dtype=np.float32))
    wu = np.ascontiguousarray(np.asarray(L0_ffn_kernel_up_EDF, dtype=np.float32))
    wd = np.ascontiguousarray(np.asarray(L0_ffn_kernel_down_EFD, dtype=np.float32))
    ident = np.eye(128, dtype=ml_dtypes.bfloat16)
    c8 = np.broadcast_to((NE - np.arange(NE, dtype=np.float32)), (128, NE)).copy()
    in_maps = []
    for c in range(NCORES):
        xs = x[c * SEQ_PER_CORE:(c + 1) * SEQ_PER_CORE].reshape(TOKENS, D)
        in_maps.append(dict(x=np.ascontiguousarray(xs), wl=wl, wr=wr, rt=rt,
                            wg=wg, wu=wu, wd=wd, ident=ident, c8=c8))
    return in_maps


def kernel(L0_attn_x, L0_attn_w_lhs, L0_attn_w_rhs, L0_ffn_router_kernel_DE,
           L0_ffn_kernel_gating_EDF, L0_ffn_kernel_up_EDF, L0_ffn_kernel_down_EFD):
    in_maps = make_inputs(L0_attn_x, L0_attn_w_lhs, L0_attn_w_rhs, L0_ffn_router_kernel_DE,
                          L0_ffn_kernel_gating_EDF, L0_ffn_kernel_up_EDF, L0_ffn_kernel_down_EFD)
    nc = _get_nc(debug=False)
    res = run_bass_kernel_spmd(nc, in_maps, core_ids=list(range(NCORES)))
    out = np.concatenate([np.asarray(r["out"], dtype=np.float32).reshape(SEQ_PER_CORE, S, D)
                          for r in res.results], axis=0)
    return out


# version 04
# speedup vs baseline: 1.1521x; 1.0003x over previous
import numpy as np
import ml_dtypes
import concourse.bass as bass
import concourse.mybir as mybir
from concourse.bass_utils import run_bass_kernel_spmd
from contextlib import ExitStack

F32 = mybir.dt.float32
BF16 = mybir.dt.bfloat16
AF = mybir.ActivationFunctionType
ALU = mybir.AluOpType
AX = mybir.AxisListType

B, S, D, FF, NE = 16, 2048, 1024, 2048, 8
NCORES = 8
SEQ_PER_CORE = B // NCORES
TOKENS = SEQ_PER_CORE * S
TG = 1024
NTG = S // TG
NDC = D // 128
NFC = FF // 128
NTC = TG // 128
LNT = S // 128
EPS = 1e-5

ENGINES = ("sync", "gpsimd", "vector", "scalar", "tensor")


class Sched:

    def __init__(self):
        self.ops = {e: [] for e in ENGINES}
        self.cnt = {}

    def op(self, eng, fn, waits=(), inc=None, inc_amt=1):
        mark = None
        if inc is not None:
            self.cnt[inc] = self.cnt.get(inc, 0) + inc_amt
            mark = self.cnt[inc]
        self.ops[eng].append((fn, tuple(w for w in waits if w[1] is not None), inc, inc_amt))
        return mark


def build_nc(debug=False):
    nc = bass.Bass("TRN2", target_bir_lowering=False)

    x_h = nc.dram_tensor("x", [TOKENS, D], F32, kind="ExternalInput").ap()
    wl_h = nc.dram_tensor("wl", [128, NDC], F32, kind="ExternalInput").ap()
    wr_h = nc.dram_tensor("wr", [128, NDC], F32, kind="ExternalInput").ap()
    rt_h = nc.dram_tensor("rt", [D, NE], F32, kind="ExternalInput").ap()
    wg_h = nc.dram_tensor("wg", [NE, D, FF], F32, kind="ExternalInput").ap()
    wu_h = nc.dram_tensor("wu", [NE, D, FF], F32, kind="ExternalInput").ap()
    wd_h = nc.dram_tensor("wd", [NE, FF, D], F32, kind="ExternalInput").ap()
    id_h = nc.dram_tensor("ident", [128, 128], BF16, kind="ExternalInput").ap()
    c8_h = nc.dram_tensor("c8", [128, NE], F32, kind="ExternalInput").ap()
    out_h = nc.dram_tensor("out", [TOKENS, D], F32, kind="ExternalOutput").ap()

    wgc = nc.dram_tensor("wgc", [NE, NFC, 128, D], BF16, kind="Internal").ap()
    wuc = nc.dram_tensor("wuc", [NE, NFC, 128, D], BF16, kind="Internal").ap()
    wdc = nc.dram_tensor("wdc", [NE, NFC, 128, D], BF16, kind="Internal").ap()

    if debug:
        dbg_xn = nc.dram_tensor("dbg_xn", [TOKENS, D], BF16, kind="ExternalOutput").ap()
        dbg_fw = nc.dram_tensor("dbg_fw", [SEQ_PER_CORE, 128, 128], F32, kind="ExternalOutput").ap()
        dbg_ls = nc.dram_tensor("dbg_ls", [SEQ_PER_CORE, 128, 64], F32, kind="ExternalOutput").ap()
        dbg_xf = nc.dram_tensor("dbg_xf", [SEQ_PER_CORE, 128, NDC * S], BF16, kind="ExternalOutput").ap()

    sch = Sched()
    SV, SS, ST, SO = "SV", "SS", "ST", "SO"
    SEM_NAMES = (["SX_C", "SX_X0", "SX_X1", "SX_A", "SX_D",
                  "SX_G0", "SX_G1", "SX_U0", "SX_U1", "SO", "SW_R", "SX_B",
                  "SV", "SS", "ST"]
                 + [f"SWb{e}{m}" for e in range(NE) for m in "gud"])

    with ExitStack() as ctx:
        E = ctx.enter_context
        xn_tm = E(nc.sbuf_tensor("xn_tm", [128, LNT * D], BF16))
        xn_fm = E(nc.sbuf_tensor("xn_fm", [128, NDC * S], BF16))
        acc = E(nc.sbuf_tensor("acc", [128, NTC * D], F32))
        h_sb = E(nc.sbuf_tensor("h_sb", [128, NFC * TG], BF16))
        wd_all = E(nc.sbuf_tensor("wd_all", [128, NFC * D], BF16))
        wg_buf = [E(nc.sbuf_tensor(f"wgb{i}", [128, D], BF16)) for i in range(2)]
        wu_buf = [E(nc.sbuf_tensor(f"wub{i}", [128, D], BF16)) for i in range(2)]
        unionA = E(nc.sbuf_tensor("unionA", [128, 3072], F32))
        sg_sb = E(nc.sbuf_tensor("sg_sb", [128, 2048], F32))
        xw_buf = [E(nc.sbuf_tensor(f"xwb{i}", [128, NDC * 128], BF16)) for i in range(2)]
        attn_sb = E(nc.sbuf_tensor("attn_sb", [128, S], BF16))
        attnT_sb = E(nc.sbuf_tensor("attnT_sb", [128, S], BF16))
        astat = E(nc.sbuf_tensor("astat", [128, 20], F32))
        lstat = E(nc.sbuf_tensor("lstat", [128, 160], F32))
        logits_sb = E(nc.sbuf_tensor("logits_sb", [128, 128], F32))
        fw_sb = E(nc.sbuf_tensor("fw_sb", [128, 128], F32))
        ra = E(nc.sbuf_tensor("ra", [128, 128], F32))
        rb = E(nc.sbuf_tensor("rb", [128, 128], F32))
        rc = E(nc.sbuf_tensor("rc", [128, 128], F32))
        rd = E(nc.sbuf_tensor("rd", [128, 128], F32))
        r16 = E(nc.sbuf_tensor("r16", [128, 288], F32))
        ident = E(nc.sbuf_tensor("ident_sb", [128, 128], BF16))
        c8 = E(nc.sbuf_tensor("c8_sb", [128, NE], F32))
        wl_sb = E(nc.sbuf_tensor("wl_sb", [128, NDC], F32))
        wr_sb = E(nc.sbuf_tensor("wr_sb", [128, NDC], F32))
        wlr = E(nc.sbuf_tensor("wlr", [128, NDC], F32))
        rt_bf = E(nc.sbuf_tensor("rt_bf", [128, NDC * NE], BF16))
        dls = E(nc.sbuf_tensor("dls", [128, 64], F32)) if debug else None

        x_t = [unionA[:, 0:1024], unionA[:, 1024:2048]]
        sqdump = unionA[:, 2048:3072]
        sg_t = [sg_sb[:, 0:1024], sg_sb[:, 1024:2048]]

        P = [E(nc.psum_tensor(f"P{i}", [128, 512], F32)) for i in range(6)]
        T = [E(nc.psum_tensor(f"T{i}", [128, 1024], BF16)) for i in range(2)]

        sems = {n: E(nc.semaphore(n)) for n in SEM_NAMES}

        bankfree = {f"P{i}": None for i in range(6)}
        bankfree.update({"T0": None, "T1": None})

        def bwait(*names):
            return [bankfree[n] for n in names if bankfree[n] is not None]

        wg_busy = [None, None]
        wu_busy = [None, None]
        sg_busy = [None, None]
        h_guard = None
        wd_free = None
        xn_guard = None
        store_mark = {}
        xn_last_prev = None
        acc_done = None

        sch.op("sync", lambda e: e.dma_start(out=ident[:], in_=id_h), inc="SX_C", inc_amt=16)
        sch.op("sync", lambda e: e.dma_start(out=c8[:], in_=c8_h), inc="SX_C", inc_amt=16)
        sch.op("sync", lambda e: e.dma_start(out=wl_sb[:], in_=wl_h), inc="SX_C", inc_amt=16)
        sch.op("sync", lambda e: e.dma_start(out=wr_sb[:], in_=wr_h), inc="SX_C", inc_amt=16)
        M_C = 64
        m_wlr = sch.op("vector", lambda e: e.tensor_tensor(wlr[:], wl_sb[:], wr_sb[:], op=ALU.mult),
                       waits=[("SX_C", M_C)], inc=SV)
        for dc in range(NDC):
            sch.op("gpsimd", lambda e, dc=dc: e.dma_start(
                out=rt_bf[:, dc * NE:(dc + 1) * NE], in_=rt_h[dc * 128:(dc + 1) * 128, :]),
                inc="SW_R", inc_amt=16)
        M_RT = 128

        build_done = {}
        for e_i in range(NE):
            sem = f"SWb{e_i}d"
            m = sch.op("gpsimd", lambda e, e_i=e_i: e.dma_start(
                out=wdc[e_i].rearrange("fc p dd -> (fc p) dd"), in_=wd_h[e_i]),
                inc=sem, inc_amt=16)
            build_done[(e_i, "d")] = (sem, m)
            for mat, src_h, dst_h in (("g", wg_h, wgc), ("u", wu_h, wuc)):
                sem = f"SWb{e_i}{mat}"
                m = None
                for dc in range(NDC):
                    for q in range(4):
                        fc0, fc1 = q * 4, (q + 1) * 4
                        m = sch.op("gpsimd", lambda e, e_i=e_i, dc=dc, fc0=fc0, fc1=fc1,
                                   src_h=src_h, dst_h=dst_h: e.dma_start(
                            out=dst_h[e_i, fc0:fc1, :, dc * 128:(dc + 1) * 128].rearrange("fc p f -> p fc f"),
                            in_=src_h[e_i, dc * 128:(dc + 1) * 128, fc0 * 128:fc1 * 128].rearrange("p (fc f) -> p fc f", f=128)),
                            inc=sem, inc_amt=16)
                build_done[(e_i, mat)] = (sem, m)

        for seq in range(SEQ_PER_CORE):
            sbase = seq * S
            rs16 = lstat[:, 0:16]
            ssq16 = lstat[:, 16:32]
            mean16 = lstat[:, 32:48]
            ssqm16 = lstat[:, 48:64]
            lm2 = lstat[:, 64:80]
            lm2b = lstat[:, 80:96]
            lvar = lstat[:, 96:112]
            lsd = lstat[:, 112:128]
            lri = lstat[:, 128:144]
            lri2 = lstat[:, 144:160]

            m_ssq = {}
            for tc in range(LNT):
                p = tc % 2
                w_ln = []
                if seq > 0 and tc < 2:
                    w_ln.append((SV, xn_last_prev))
                if tc >= 2:
                    w_ln.append((SV, m_ssq[tc - 2]))
                ldq = "sync" if seq == 0 else "gpsimd"
                m_x = sch.op(ldq, lambda e, tc=tc, p=p, sbase=sbase: e.dma_start(
                    out=x_t[p], in_=x_h[sbase + tc * 128: sbase + (tc + 1) * 128, :]),
                    waits=w_ln, inc=f"SX_X{p}", inc_amt=16)
                sch.op("vector", lambda e, tc=tc, p=p: e.reduce_sum(
                    rs16[:, tc:tc + 1], x_t[p], axis=AX.X), waits=[(f"SX_X{p}", m_x)])
                sch.op("vector", lambda e, p=p: e.tensor_tensor(
                    sqdump, x_t[p], x_t[p], op=ALU.mult))
                m_ssq[tc] = sch.op("vector", lambda e, tc=tc: e.reduce_sum(
                    ssq16[:, tc:tc + 1], sqdump, axis=AX.X), inc=SV)

            sch.op("scalar", lambda e: e.mul(mean16, rs16, 1.0 / D), waits=[(SV, m_ssq[LNT - 1])])
            m_l2 = sch.op("scalar", lambda e: e.mul(ssqm16, ssq16, 1.0 / D), inc=SS)
            m_m2 = sch.op("vector", lambda e: e.tensor_tensor(lm2, mean16, mean16, op=ALU.mult),
                          waits=[(SS, m_l2)], inc=SV)
            m_m2b = sch.op("scalar", lambda e: e.copy(lm2b, lm2), waits=[(SV, m_m2)], inc=SS)
            m_var = sch.op("vector", lambda e: e.scalar_tensor_tensor(
                lvar, ssqm16, 1.0, lm2b, op0=ALU.mult, op1=ALU.subtract),
                waits=[(SS, m_m2b)], inc=SV)
            m_sdm = sch.op("scalar", lambda e: e.activation(lsd, lvar, AF.Sqrt, bias=np.float32(EPS)),
                           waits=[(SV, m_var)], inc=SS)
            m_ri = sch.op("vector", lambda e: e.reciprocal(lri, lsd), waits=[(SS, m_sdm)], inc=SV)
            m_ri2 = sch.op("scalar", lambda e: e.copy(lri2, lri), waits=[(SV, m_ri)], inc=SS)
            if debug:
                sch.op("scalar", lambda e: e.copy(dls[:, 0:16], mean16))
                sch.op("scalar", lambda e: e.copy(dls[:, 32:48], lri2))

            xn_done = None
            m_xn = {}
            m_lc = None
            for tc in range(LNT):
                p = tc % 2
                w3 = [(SV, m_ssq[LNT - 2 + p])] if tc < 2 else [(SV, m_xn[tc - 2])]
                ldq = "sync" if seq == 0 else "gpsimd"
                m_x = sch.op(ldq, lambda e, tc=tc, p=p, sbase=sbase: e.dma_start(
                    out=x_t[p], in_=x_h[sbase + tc * 128: sbase + (tc + 1) * 128, :]),
                    waits=w3, inc=f"SX_X{p}", inc_amt=16)
                w_xn = [(f"SX_X{p}", m_x), (SS, m_ri2)]
                if seq > 0 and tc == 0:
                    w_xn.append((ST, xn_guard))
                if debug and seq > 0 and tc == 0:
                    w_xn.append(("SX_B", 64 * seq))
                m_xn[tc] = sch.op("vector", lambda e, p=p, tc=tc: e.tensor_scalar(
                    xn_tm[:, tc * D:(tc + 1) * D], x_t[p], mean16[:, tc:tc + 1], lri2[:, tc:tc + 1],
                    op0=ALU.subtract, op1=ALU.mult), waits=w_xn, inc=SV)
                tb = f"T{p}"
                m_tr = None
                for dc in range(NDC):
                    m_tr = sch.op("tensor", lambda e, p=p, tc=tc, dc=dc: e.transpose(
                        T[p][:, dc * 128:(dc + 1) * 128],
                        xn_tm[:, tc * D + dc * 128: tc * D + (dc + 1) * 128], ident[:]),
                        waits=([(SV, m_xn[tc]), ("SX_C", M_C)] + (bwait(tb) if dc == 0 else [])),
                        inc=(ST if dc == NDC - 1 else None))
                m_ev = sch.op("vector", lambda e, p=p, tc=tc: e.tensor_copy(
                    xn_fm[:].rearrange("p (dc t) -> p dc t", dc=NDC)[:, :, tc * 128:(tc + 1) * 128],
                    T[p][:].rearrange("p (dc t) -> p dc t", dc=NDC)),
                    waits=[(ST, m_tr)], inc=SV)
                bankfree[tb] = (SV, m_ev)
                xn_done = m_ev
                m_lg = None
                for dc in range(NDC):
                    m_lg = sch.op("tensor", lambda e, tc=tc, dc=dc: e.matmul(
                        P[4][:, 0:NE],
                        xn_fm[:, dc * S + tc * 128: dc * S + (tc + 1) * 128],
                        rt_bf[:, dc * NE:(dc + 1) * NE],
                        start=(dc == 0), stop=(dc == NDC - 1)),
                        waits=([(SV, m_ev), ("SW_R", M_RT)] + (bwait("P4") if dc == 0 else [])),
                        inc=(ST if dc == NDC - 1 else None))
                m_lc = sch.op("scalar", lambda e, tc=tc: e.copy(
                    logits_sb[:, tc * NE:(tc + 1) * NE], P[4][:, 0:NE]),
                    waits=[(ST, m_lg)], inc=SS)
                bankfree["P4"] = (SS, m_lc)
            xn_last_prev = m_xn[LNT - 1]

            def a3(t):
                return t[:].rearrange("p (t e) -> p t e", e=NE)

            def bc16(ap16):
                return ap16.unsqueeze(2).broadcast_to([128, 16, NE])

            def bc8(ap8):
                return ap8.unsqueeze(1).broadcast_to([128, 16, NE])

            nmax = r16[:, 0:16]
            nmaxb = r16[:, 16:32]
            rsum = r16[:, 32:48]
            rsumb = r16[:, 48:64]
            rrec = r16[:, 64:80]
            rrecb = r16[:, 80:96]
            m1 = r16[:, 96:112]
            m1b = r16[:, 112:128]
            m2 = r16[:, 128:144]
            m2b = r16[:, 144:160]
            c1 = r16[:, 160:176]
            c1b = r16[:, 176:192]
            c2 = r16[:, 192:208]
            c2b = r16[:, 208:224]
            d21 = r16[:, 224:240]
            nw1 = r16[:, 240:256]
            nw2 = r16[:, 256:272]
            mv = sch.op("vector", lambda e: e.tensor_reduce(nmax, a3(logits_sb), axis=AX.X, op=ALU.max, negate=True),
                        waits=[(SS, m_lc)], inc=SV)
            ms = sch.op("scalar", lambda e: e.copy(nmaxb, nmax), waits=[(SV, mv)], inc=SS)
            mv = sch.op("vector", lambda e: e.tensor_tensor(a3(rb), a3(logits_sb), bc16(nmaxb), op=ALU.add),
                        waits=[(SS, ms)], inc=SV)
            ms = sch.op("scalar", lambda e: e.activation(ra[:], rb[:], AF.Exp), waits=[(SV, mv)], inc=SS)
            mv = sch.op("vector", lambda e: e.tensor_reduce(rsum, a3(ra), axis=AX.X, op=ALU.add),
                        waits=[(SS, ms)], inc=SV)
            ms = sch.op("scalar", lambda e: e.copy(rsumb, rsum), waits=[(SV, mv)], inc=SS)
            mv = sch.op("vector", lambda e: e.reciprocal(rrec, rsumb), waits=[(SS, ms)], inc=SV)
            ms = sch.op("scalar", lambda e: e.copy(rrecb, rrec), waits=[(SV, mv)], inc=SS)
            sch.op("vector", lambda e: e.tensor_tensor(a3(rb), a3(ra), bc16(rrecb), op=ALU.mult),
                   waits=[(SS, ms)])
            mv = sch.op("vector", lambda e: e.tensor_reduce(m1, a3(rb), axis=AX.X, op=ALU.max), inc=SV)
            ms = sch.op("scalar", lambda e: e.copy(m1b, m1), waits=[(SV, mv)], inc=SS)
            sch.op("vector", lambda e: e.tensor_tensor(a3(rc), a3(rb), bc16(m1b), op=ALU.is_ge),
                   waits=[(SS, ms)])
            sch.op("vector", lambda e: e.tensor_tensor(a3(rd), a3(rc), bc8(c8[:]), op=ALU.mult))
            mv = sch.op("vector", lambda e: e.tensor_reduce(c1, a3(rd), axis=AX.X, op=ALU.max), inc=SV)
            ms = sch.op("scalar", lambda e: e.copy(c1b, c1), waits=[(SV, mv)], inc=SS)
            sch.op("vector", lambda e: e.tensor_tensor(a3(rc), a3(rd), bc16(c1b), op=ALU.is_ge),
                   waits=[(SS, ms)])
            sch.op("vector", lambda e: e.scalar_tensor_tensor(a3(rd), a3(rc), -2.0, a3(rb), op0=ALU.mult, op1=ALU.add))
            mv = sch.op("vector", lambda e: e.tensor_reduce(m2, a3(rd), axis=AX.X, op=ALU.max), inc=SV)
            ms = sch.op("scalar", lambda e: e.copy(m2b, m2), waits=[(SV, mv)], inc=SS)
            sch.op("vector", lambda e: e.tensor_tensor(a3(ra), a3(rd), bc16(m2b), op=ALU.is_ge),
                   waits=[(SS, ms)])
            sch.op("vector", lambda e: e.tensor_tensor(a3(rd), a3(ra), bc8(c8[:]), op=ALU.mult))
            mv = sch.op("vector", lambda e: e.tensor_reduce(c2, a3(rd), axis=AX.X, op=ALU.max), inc=SV)
            ms = sch.op("scalar", lambda e: e.copy(c2b, c2), waits=[(SV, mv)], inc=SS)
            sch.op("vector", lambda e: e.tensor_tensor(a3(ra), a3(rd), bc16(c2b), op=ALU.is_ge),
                   waits=[(SS, ms)])
            mv = sch.op("vector", lambda e: e.tensor_tensor(d21, m2b, m1b, op=ALU.subtract), inc=SV)
            sch.op("scalar", lambda e: e.activation(nw2, d21, AF.Sigmoid), waits=[(SV, mv)])
            ms = sch.op("scalar", lambda e: e.activation(nw1, d21, AF.Sigmoid, scale=np.float32(-1.0)), inc=SS)
            sch.op("vector", lambda e: e.tensor_tensor(a3(rb), a3(rc), bc16(nw1), op=ALU.mult),
                   waits=[(SS, ms)])
            sch.op("vector", lambda e: e.tensor_tensor(a3(rd), a3(ra), bc16(nw2), op=ALU.mult))
            m_fw = sch.op("vector", lambda e: e.tensor_tensor(fw_sb[:], rb[:], rd[:], op=ALU.add), inc=SV)
            sg_busy = [xn_done, xn_done]

            if debug:
                sch.op("sync", lambda e, sbase=sbase: e.dma_start(
                    out=dbg_xn[sbase:sbase + S, :].rearrange("(tc p) d -> p tc d", p=128),
                    in_=xn_tm[:].rearrange("p (tc d) -> p tc d", tc=LNT)),
                    waits=[(SV, xn_done)], inc="SX_B", inc_amt=16)
                sch.op("sync", lambda e, seq=seq: e.dma_start(
                    out=dbg_fw[seq], in_=fw_sb[:]), waits=[(SV, m_fw)], inc="SX_B", inc_amt=16)
                sch.op("sync", lambda e, seq=seq: e.dma_start(
                    out=dbg_ls[seq], in_=dls[:]), inc="SX_B", inc_amt=16)
                sch.op("sync", lambda e, seq=seq: e.dma_start(
                    out=dbg_xf[seq], in_=xn_fm[:]), inc="SX_B", inc_amt=16)

            for tg in range(NTG):
                tgg = seq * NTG + tg
                tbase = sbase + tg * TG
                m_ai = None
                for tc in range(NTC):
                    w_ai = [(SO, store_mark.get(tgg - 1))] if tc == 0 else []
                    m_ai = sch.op("sync", lambda e, tc=tc, tbase=tbase: e.dma_start(
                        out=acc[:, tc * D:(tc + 1) * D],
                        in_=x_h[tbase + tc * 128: tbase + (tc + 1) * 128, :]),
                        waits=w_ai, inc="SX_A", inc_amt=16)

                at = {}
                for qc in range(NTC + 1):
                    if qc < NTC:
                        qt = tg * NTC + qc
                        pp = qc % 2
                        m_xw = None
                        for dc in range(NDC):
                            m_xw = sch.op("vector", lambda e, dc=dc, pp=pp, qt=qt: e.tensor_scalar(
                                xw_buf[pp][:, dc * 128:(dc + 1) * 128],
                                xn_fm[:, dc * S + qt * 128: dc * S + (qt + 1) * 128],
                                wlr[:, dc:dc + 1], 0.0, op0=ALU.mult, op1=ALU.add),
                                waits=[(SV, m_wlr), (SV, xn_done)],
                                inc=(SV if dc == NDC - 1 else None))
                        m_sc = None
                        for dc in range(NDC):
                            for kt in range(4):
                                m_sc = sch.op("tensor", lambda e, dc=dc, kt=kt, pp=pp: e.matmul(
                                    P[kt][:],
                                    xw_buf[pp][:, dc * 128:(dc + 1) * 128],
                                    xn_fm[:, dc * S + kt * 512: dc * S + (kt + 1) * 512],
                                    start=(dc == 0), stop=(dc == NDC - 1)),
                                    waits=([(SV, m_xw)] + (bwait("P0", "P1", "P2", "P3") if dc == 0 else [])),
                                    inc=(ST if (dc == NDC - 1 and kt == 3) else None))
                        at[(qc, "sc")] = m_sc
                    if qc >= 1:
                        q1 = qc - 1
                        m_tr = [None, None]
                        for half in range(2):
                            for j in range(8):
                                kc = half * 8 + j
                                m_tr[half] = sch.op("tensor", lambda e, half=half, j=j, kc=kc: e.transpose(
                                    T[half][:, j * 128:(j + 1) * 128],
                                    attn_sb[:, kc * 128:(kc + 1) * 128], ident[:]),
                                    waits=([(SS, at[(q1, "exp")])] + (bwait(f"T{half}") if j == 0 else [])),
                                    inc=(ST if j == 7 else None))
                        at[(q1, "tr")] = m_tr
                    if qc < NTC:
                        m_m4 = None
                        for kt in range(4):
                            m_m4 = sch.op("vector", lambda e, kt=kt: e.reduce_max(
                                astat[:, kt:kt + 1], P[kt][:], axis=AX.X), waits=[(ST, at[(qc, "sc")])],
                                inc=(SV if kt == 3 else None))
                        m_m4b = sch.op("scalar", lambda e: e.copy(astat[:, 4:8], astat[:, 0:4]),
                                       waits=[(SV, m_m4)], inc=SS)
                        at[(qc, "nm")] = sch.op("vector", lambda e: e.reduce_max(
                            astat[:, 8:9], astat[:, 4:8], axis=AX.X, negate=True),
                            waits=[(SS, m_m4b)], inc=SV)
                    if qc >= 1:
                        q1 = qc - 1
                        m_tr = at[(q1, "tr")]
                        m_te = [None, None]
                        for half in range(2):
                            m_te[half] = sch.op("vector", lambda e, half=half: e.tensor_copy(
                                attnT_sb[:, half * 1024:(half + 1) * 1024], T[half][:]),
                                waits=[(ST, m_tr[half])], inc=SV)
                            bankfree[f"T{half}"] = (SV, m_te[half])
                        m_av = None
                        for kc in range(16):
                            for dh in range(2):
                                m_av = sch.op("tensor", lambda e, kc=kc, dh=dh: e.matmul(
                                    P[4 + dh][:],
                                    attnT_sb[:, kc * 128:(kc + 1) * 128],
                                    xn_tm[:, kc * D + dh * 512: kc * D + (dh + 1) * 512],
                                    start=(kc == 0), stop=(kc == 15)),
                                    waits=([(SV, m_te[kc // 8])] + (bwait("P4", "P5") if kc == 0 else [])),
                                    inc=(ST if (kc == 15 and dh == 1) else None))
                        at[(q1, "av")] = m_av
                    if qc < NTC:
                        m_exp = None
                        w_exp0 = [(ST, at[(qc - 1, "tr")][1])] if qc >= 1 else []
                        for kt in range(4):
                            m_exp = sch.op("scalar", lambda e, kt=kt: e.activation(
                                attn_sb[:, kt * 512:(kt + 1) * 512], P[kt][:], AF.Exp,
                                bias=astat[:, 8:9], accum_out=astat[:, 9 + kt:10 + kt]),
                                waits=[(SV, at[(qc, "nm")])] + (w_exp0 if kt == 0 else []),
                                inc=(SS if kt == 3 else None))
                        at[(qc, "exp")] = m_exp
                        for i in range(4):
                            bankfree[f"P{i}"] = (SS, m_exp)
                        rp = qc % 2
                        m_su = sch.op("vector", lambda e: e.reduce_sum(
                            astat[:, 13:14], astat[:, 9:13], axis=AX.X), waits=[(SS, m_exp)], inc=SV)
                        m_sb = sch.op("scalar", lambda e: e.copy(astat[:, 14:15], astat[:, 13:14]),
                                      waits=[(SV, m_su)], inc=SS)
                        m_rc2 = sch.op("vector", lambda e, rp=rp: e.reciprocal(
                            astat[:, 15 + 2 * rp:16 + 2 * rp], astat[:, 14:15]),
                            waits=[(SS, m_sb)], inc=SV)
                        at[(qc, "recb")] = sch.op("scalar", lambda e, rp=rp: e.copy(
                            astat[:, 16 + 2 * rp:17 + 2 * rp], astat[:, 15 + 2 * rp:16 + 2 * rp]),
                            waits=[(SV, m_rc2)], inc=SS)
                    if qc >= 1:
                        q1 = qc - 1
                        rp = q1 % 2
                        m_ae = None
                        for dh in range(2):
                            m_ae = sch.op("vector", lambda e, dh=dh, q1=q1, rp=rp: e.scalar_tensor_tensor(
                                acc[:, q1 * D + dh * 512: q1 * D + (dh + 1) * 512],
                                P[4 + dh][:], astat[:, 16 + 2 * rp:17 + 2 * rp],
                                acc[:, q1 * D + dh * 512: q1 * D + (dh + 1) * 512],
                                op0=ALU.mult, op1=ALU.add),
                                waits=[(ST, at[(q1, "av")]), (SS, at[(q1, "recb")]), ("SX_A", m_ai)], inc=SV)
                        bankfree["P4"] = (SV, m_ae)
                        bankfree["P5"] = (SV, m_ae)

                pending_down = None

                def emit_down(e_d, m_hm_d, m_wd_d, odd_pair):
                    nonlocal h_guard, wd_free, xn_guard, acc_done
                    m_dn_last = None
                    m_de = None
                    for tc in range(NTC):
                        dpair = (4, 5) if tc % 2 == 0 else odd_pair
                        m_d = None
                        for fc in range(NFC):
                            for ddh in range(2):
                                m_d = sch.op("tensor", lambda e, fc=fc, ddh=ddh, tc=tc, dpair=dpair: e.matmul(
                                    P[dpair[ddh]][:],
                                    h_sb[:, fc * TG + tc * 128: fc * TG + (tc + 1) * 128],
                                    wd_all[:, fc * D + ddh * 512: fc * D + (ddh + 1) * 512],
                                    start=(fc == 0), stop=(fc == NFC - 1)),
                                    waits=([(SV, m_hm_d), ("SX_D", m_wd_d)] + (bwait(f"P{dpair[0]}", f"P{dpair[1]}") if fc == 0 else [])),
                                    inc=(ST if (fc == NFC - 1 and ddh == 1) else None))
                        m_dn_last = m_d
                        fcol = (tg * NTC + tc) * NE + e_d
                        for ddh in range(2):
                            m_de = sch.op("vector", lambda e, ddh=ddh, tc=tc, dpair=dpair, fcol=fcol: e.scalar_tensor_tensor(
                                acc[:, tc * D + ddh * 512: tc * D + (ddh + 1) * 512],
                                P[dpair[ddh]][:], fw_sb[:, fcol:fcol + 1],
                                acc[:, tc * D + ddh * 512: tc * D + (ddh + 1) * 512],
                                op0=ALU.mult, op1=ALU.add),
                                waits=[(ST, m_d), (SV, m_fw)], inc=SV)
                        bankfree[f"P{dpair[0]}"] = (SV, m_de)
                        bankfree[f"P{dpair[1]}"] = (SV, m_de)
                    h_guard = m_dn_last
                    wd_free = m_dn_last
                    xn_guard = m_dn_last
                    acc_done = m_de

                for e_i in range(NE):
                    m_wg = {}
                    m_wu = {}

                    def load_wg(fc):
                        pp = fc % 2
                        m_wg[fc] = sch.op("sync", lambda e, e_i=e_i, fc=fc, pp=pp: e.dma_start(
                            out=wg_buf[pp][:], in_=wgc[e_i, fc]),
                            waits=[build_done[(e_i, "g")], (ST, wg_busy[pp])],
                            inc=f"SX_G{pp}", inc_amt=16)

                    def load_wu(fc):
                        pp = fc % 2
                        m_wu[fc] = sch.op("sync", lambda e, e_i=e_i, fc=fc, pp=pp: e.dma_start(
                            out=wu_buf[pp][:], in_=wuc[e_i, fc]),
                            waits=[build_done[(e_i, "u")], (ST, wu_busy[pp])],
                            inc=f"SX_U{pp}", inc_amt=16)

                    def load_wd():
                        m = None
                        for fc in range(NFC):
                            m = sch.op("sync", lambda e, e_i=e_i, fc=fc: e.dma_start(
                                out=wd_all[:, fc * D:(fc + 1) * D], in_=wdc[e_i, fc]),
                                waits=[build_done[(e_i, "d")]] + ([(ST, wd_free)] if fc == 0 else []),
                                inc="SX_D", inc_amt=16)
                        return m

                    for fc in (0, 1):
                        load_wg(fc)
                        load_wu(fc)
                    if pending_down is None:
                        m_wd = load_wd()

                    m_hm_last = None
                    for fc in range(NFC):
                        pp = fc % 2
                        gpair = (0, 1)
                        upair = (2, 3)
                        m_g = None
                        for dc in range(NDC):
                            for tt in range(2):
                                m_g = sch.op("tensor", lambda e, dc=dc, tt=tt, pp=pp, gpair=gpair, tg=tg: e.matmul(
                                    P[gpair[tt]][:],
                                    wg_buf[pp][:, dc * 128:(dc + 1) * 128],
                                    xn_fm[:, dc * S + tg * TG + tt * 512: dc * S + tg * TG + (tt + 1) * 512],
                                    start=(dc == 0), stop=(dc == NDC - 1)),
                                    waits=([(f"SX_G{pp}", m_wg[fc])] + (bwait(f"P{gpair[0]}", f"P{gpair[1]}") if dc == 0 else [])),
                                    inc=(ST if (dc == NDC - 1 and tt == 1) else None))
                        wg_busy[pp] = m_g
                        if fc + 2 < NFC:
                            load_wg(fc + 2)
                        m_si = None
                        for tt in range(2):
                            m_si = sch.op("scalar", lambda e, tt=tt, pp=pp, gpair=gpair: e.activation(
                                sg_t[pp][:, tt * 512:(tt + 1) * 512], P[gpair[tt]][:], AF.Silu),
                                waits=[(ST, m_g), (SV, sg_busy[pp])],
                                inc=(SS if tt == 1 else None))
                        bankfree[f"P{gpair[0]}"] = (SS, m_si)
                        bankfree[f"P{gpair[1]}"] = (SS, m_si)
                        m_u = None
                        for dc in range(NDC):
                            for tt in range(2):
                                m_u = sch.op("tensor", lambda e, dc=dc, tt=tt, pp=pp, upair=upair, tg=tg: e.matmul(
                                    P[upair[tt]][:],
                                    wu_buf[pp][:, dc * 128:(dc + 1) * 128],
                                    xn_fm[:, dc * S + tg * TG + tt * 512: dc * S + tg * TG + (tt + 1) * 512],
                                    start=(dc == 0), stop=(dc == NDC - 1)),
                                    waits=([(f"SX_U{pp}", m_wu[fc])] + (bwait(f"P{upair[0]}", f"P{upair[1]}") if dc == 0 else [])),
                                    inc=(ST if (dc == NDC - 1 and tt == 1) else None))
                        wu_busy[pp] = m_u
                        if fc + 2 < NFC:
                            load_wu(fc + 2)
                        if fc == 0 and pending_down is not None:
                            pending_down()
                            m_wd = load_wd()
                        m_hm = None
                        w_h = [(ST, h_guard)] if (h_guard is not None and fc == 0) else []
                        for tt in range(2):
                            m_hm = sch.op("vector", lambda e, tt=tt, pp=pp, fc=fc, upair=upair: e.tensor_tensor(
                                h_sb[:, fc * TG + tt * 512: fc * TG + (tt + 1) * 512],
                                sg_t[pp][:, tt * 512:(tt + 1) * 512], P[upair[tt]][:], op=ALU.mult),
                                waits=[(SS, m_si), (ST, m_u)] + (w_h if tt == 0 else []),
                                inc=(SV if tt == 1 else None))
                        sg_busy[pp] = m_hm
                        bankfree[f"P{upair[0]}"] = (SV, m_hm)
                        bankfree[f"P{upair[1]}"] = (SV, m_hm)
                        m_hm_last = m_hm

                    pending_down = (lambda e_d=e_i, m_hm_d=m_hm_last, m_wd_d=m_wd:
                                    emit_down(e_d, m_hm_d, m_wd_d, (0, 1)))

                emit_down(NE - 1, m_hm_last, m_wd, (2, 3))
                pending_down = None

                m_st = None
                for tc in range(NTC):
                    m_st = sch.op("sync", lambda e, tc=tc, tbase=tbase: e.dma_start(
                        out=out_h[tbase + tc * 128: tbase + (tc + 1) * 128, :],
                        in_=acc[:, tc * D:(tc + 1) * D]),
                        waits=[(SV, acc_done)], inc=SO, inc_amt=16)
                store_mark[tgg] = m_st

        total_so = sch.cnt[SO]
        sch.op("sync", lambda e: e.nop(), waits=[(SO, total_so)])

        with nc.Block() as block:
            for eng_name in ENGINES:
                ops = sch.ops[eng_name]

                def make(ops):
                    def emitter(eng):
                        waited = {}
                        for fn, waits, inc, inc_amt in ops:
                            for s, v in waits:
                                if waited.get(s, 0) < v:
                                    eng.wait_ge(sems[s], v)
                                    waited[s] = v
                            inst = fn(eng)
                            if inc is not None:
                                inst.then_inc(sems[inc], inc_amt)
                    return emitter

                getattr(block, eng_name)(make(ops))

    return nc


_NC_CACHE = {}


def _get_nc(debug=False):
    if debug not in _NC_CACHE:
        _NC_CACHE[debug] = build_nc(debug)
    return _NC_CACHE[debug]


def make_inputs(L0_attn_x, L0_attn_w_lhs, L0_attn_w_rhs, L0_ffn_router_kernel_DE,
                L0_ffn_kernel_gating_EDF, L0_ffn_kernel_up_EDF, L0_ffn_kernel_down_EFD):
    x = np.ascontiguousarray(np.asarray(L0_attn_x, dtype=np.float32))
    wl = np.ascontiguousarray(np.asarray(L0_attn_w_lhs, dtype=np.float32).reshape(NDC, 128).T)
    wr = np.ascontiguousarray(np.asarray(L0_attn_w_rhs, dtype=np.float32).reshape(NDC, 128).T)
    rt = np.ascontiguousarray(np.asarray(L0_ffn_router_kernel_DE, dtype=np.float32))
    wg = np.ascontiguousarray(np.asarray(L0_ffn_kernel_gating_EDF, dtype=np.float32))
    wu = np.ascontiguousarray(np.asarray(L0_ffn_kernel_up_EDF, dtype=np.float32))
    wd = np.ascontiguousarray(np.asarray(L0_ffn_kernel_down_EFD, dtype=np.float32))
    ident = np.eye(128, dtype=ml_dtypes.bfloat16)
    c8 = np.broadcast_to((NE - np.arange(NE, dtype=np.float32)), (128, NE)).copy()
    in_maps = []
    for c in range(NCORES):
        xs = x[c * SEQ_PER_CORE:(c + 1) * SEQ_PER_CORE].reshape(TOKENS, D)
        in_maps.append(dict(x=np.ascontiguousarray(xs), wl=wl, wr=wr, rt=rt,
                            wg=wg, wu=wu, wd=wd, ident=ident, c8=c8))
    return in_maps


def kernel(L0_attn_x, L0_attn_w_lhs, L0_attn_w_rhs, L0_ffn_router_kernel_DE,
           L0_ffn_kernel_gating_EDF, L0_ffn_kernel_up_EDF, L0_ffn_kernel_down_EFD):
    in_maps = make_inputs(L0_attn_x, L0_attn_w_lhs, L0_attn_w_rhs, L0_ffn_router_kernel_DE,
                          L0_ffn_kernel_gating_EDF, L0_ffn_kernel_up_EDF, L0_ffn_kernel_down_EFD)
    nc = _get_nc(debug=False)
    res = run_bass_kernel_spmd(nc, in_maps, core_ids=list(range(NCORES)))
    out = np.concatenate([np.asarray(r["out"], dtype=np.float32).reshape(SEQ_PER_CORE, S, D)
                          for r in res.results], axis=0)
    return out


# version 06
# speedup vs baseline: 1.1546x; 1.0022x over previous
import numpy as np
import ml_dtypes
import concourse.bass as bass
import concourse.mybir as mybir
from concourse.bass_utils import run_bass_kernel_spmd
from contextlib import ExitStack

F32 = mybir.dt.float32
BF16 = mybir.dt.bfloat16
AF = mybir.ActivationFunctionType
ALU = mybir.AluOpType
AX = mybir.AxisListType

B, S, D, FF, NE = 16, 2048, 1024, 2048, 8
NCORES = 8
SEQ_PER_CORE = B // NCORES
TOKENS = SEQ_PER_CORE * S
TG = 1024
NTG = S // TG
NDC = D // 128
NFC = FF // 128
NTC = TG // 128
LNT = S // 128
EPS = 1e-5

ENGINES = ("sync", "gpsimd", "vector", "scalar", "tensor")


class Sched:

    def __init__(self):
        self.ops = {e: [] for e in ENGINES}
        self.cnt = {}

    def op(self, eng, fn, waits=(), inc=None, inc_amt=1):
        mark = None
        if inc is not None:
            self.cnt[inc] = self.cnt.get(inc, 0) + inc_amt
            mark = self.cnt[inc]
        self.ops[eng].append((fn, tuple(w for w in waits if w[1] is not None), inc, inc_amt))
        return mark


def build_nc(debug=False):
    nc = bass.Bass("TRN2", target_bir_lowering=False)

    x_h = nc.dram_tensor("x", [TOKENS, D], F32, kind="ExternalInput").ap()
    wl_h = nc.dram_tensor("wl", [128, NDC], F32, kind="ExternalInput").ap()
    wr_h = nc.dram_tensor("wr", [128, NDC], F32, kind="ExternalInput").ap()
    rt_h = nc.dram_tensor("rt", [D, NE], F32, kind="ExternalInput").ap()
    wg_h = nc.dram_tensor("wg", [NE, D, FF], F32, kind="ExternalInput").ap()
    wu_h = nc.dram_tensor("wu", [NE, D, FF], F32, kind="ExternalInput").ap()
    wd_h = nc.dram_tensor("wd", [NE, FF, D], F32, kind="ExternalInput").ap()
    id_h = nc.dram_tensor("ident", [128, 128], BF16, kind="ExternalInput").ap()
    c8_h = nc.dram_tensor("c8", [128, NE], F32, kind="ExternalInput").ap()
    out_h = nc.dram_tensor("out", [TOKENS, D], F32, kind="ExternalOutput").ap()

    wgc = nc.dram_tensor("wgc", [NE, NFC, 128, D], BF16, kind="Internal").ap()
    wuc = nc.dram_tensor("wuc", [NE, NFC, 128, D], BF16, kind="Internal").ap()
    wdc = nc.dram_tensor("wdc", [NE, NFC, 128, D], BF16, kind="Internal").ap()

    if debug:
        dbg_xn = nc.dram_tensor("dbg_xn", [TOKENS, D], BF16, kind="ExternalOutput").ap()
        dbg_fw = nc.dram_tensor("dbg_fw", [SEQ_PER_CORE, 128, 128], F32, kind="ExternalOutput").ap()
        dbg_ls = nc.dram_tensor("dbg_ls", [SEQ_PER_CORE, 128, 64], F32, kind="ExternalOutput").ap()
        dbg_xf = nc.dram_tensor("dbg_xf", [SEQ_PER_CORE, 128, NDC * S], BF16, kind="ExternalOutput").ap()

    sch = Sched()
    SV, SS, ST, SO = "SV", "SS", "ST", "SO"
    SEM_NAMES = (["SX_C", "SX_X0", "SX_X1", "SX_A", "SX_D",
                  "SX_G0", "SX_G1", "SX_U0", "SX_U1", "SO", "SW_R", "SX_B",
                  "SV", "SS", "ST"]
                 + [f"SWb{e}{m}" for e in range(NE) for m in "gud"])

    with ExitStack() as ctx:
        E = ctx.enter_context
        xn_tm = E(nc.sbuf_tensor("xn_tm", [128, LNT * D], BF16))
        xn_fm = E(nc.sbuf_tensor("xn_fm", [128, NDC * S], BF16))
        acc = E(nc.sbuf_tensor("acc", [128, NTC * D], F32))
        h_sb = E(nc.sbuf_tensor("h_sb", [128, NFC * TG], BF16))
        wd_all = E(nc.sbuf_tensor("wd_all", [128, NFC * D], BF16))
        wg_buf = [E(nc.sbuf_tensor(f"wgb{i}", [128, D], BF16)) for i in range(2)]
        wu_buf = [E(nc.sbuf_tensor(f"wub{i}", [128, D], BF16)) for i in range(2)]
        unionA = E(nc.sbuf_tensor("unionA", [128, 3072], F32))
        sg_sb = E(nc.sbuf_tensor("sg_sb", [128, 2048], F32))
        xw_buf = [E(nc.sbuf_tensor(f"xwb{i}", [128, NDC * 128], BF16)) for i in range(2)]
        attn_sb = E(nc.sbuf_tensor("attn_sb", [128, S], BF16))
        attnT_sb = E(nc.sbuf_tensor("attnT_sb", [128, S], BF16))
        astat = E(nc.sbuf_tensor("astat", [128, 20], F32))
        lstat = E(nc.sbuf_tensor("lstat", [128, 160], F32))
        logits_sb = E(nc.sbuf_tensor("logits_sb", [128, 128], F32))
        fw_sb = E(nc.sbuf_tensor("fw_sb", [128, 128], F32))
        ra = E(nc.sbuf_tensor("ra", [128, 128], F32))
        rb = E(nc.sbuf_tensor("rb", [128, 128], F32))
        rc = E(nc.sbuf_tensor("rc", [128, 128], F32))
        rd = E(nc.sbuf_tensor("rd", [128, 128], F32))
        r16 = E(nc.sbuf_tensor("r16", [128, 288], F32))
        ident = E(nc.sbuf_tensor("ident_sb", [128, 128], BF16))
        c8 = E(nc.sbuf_tensor("c8_sb", [128, NE], F32))
        wl_sb = E(nc.sbuf_tensor("wl_sb", [128, NDC], F32))
        wr_sb = E(nc.sbuf_tensor("wr_sb", [128, NDC], F32))
        wlr = E(nc.sbuf_tensor("wlr", [128, NDC], F32))
        rt_bf = E(nc.sbuf_tensor("rt_bf", [128, NDC * NE], BF16))
        dls = E(nc.sbuf_tensor("dls", [128, 64], F32)) if debug else None

        x_t = [unionA[:, 0:1024], unionA[:, 1024:2048]]
        sqdump = unionA[:, 2048:3072]
        sg_t = [sg_sb[:, 0:1024], sg_sb[:, 1024:2048]]

        P = [E(nc.psum_tensor(f"P{i}", [128, 512], F32)) for i in range(6)]
        T = [E(nc.psum_tensor(f"T{i}", [128, 1024], BF16)) for i in range(2)]

        sems = {n: E(nc.semaphore(n)) for n in SEM_NAMES}

        bankfree = {f"P{i}": None for i in range(6)}
        bankfree.update({"T0": None, "T1": None})

        def bwait(*names):
            return [bankfree[n] for n in names if bankfree[n] is not None]

        wg_busy = [None, None]
        wu_busy = [None, None]
        sg_busy = [None, None]
        h_guard = None
        wd_free = None
        xn_guard = None
        store_mark = {}
        xn_last_prev = None
        acc_done = None

        sch.op("sync", lambda e: e.dma_start(out=ident[:], in_=id_h), inc="SX_C", inc_amt=16)
        sch.op("sync", lambda e: e.dma_start(out=c8[:], in_=c8_h), inc="SX_C", inc_amt=16)
        sch.op("sync", lambda e: e.dma_start(out=wl_sb[:], in_=wl_h), inc="SX_C", inc_amt=16)
        sch.op("sync", lambda e: e.dma_start(out=wr_sb[:], in_=wr_h), inc="SX_C", inc_amt=16)
        M_C = 64
        m_wlr = sch.op("vector", lambda e: e.tensor_tensor(wlr[:], wl_sb[:], wr_sb[:], op=ALU.mult),
                       waits=[("SX_C", M_C)], inc=SV)
        for dc in range(NDC):
            sch.op("gpsimd", lambda e, dc=dc: e.dma_start(
                out=rt_bf[:, dc * NE:(dc + 1) * NE], in_=rt_h[dc * 128:(dc + 1) * 128, :]),
                inc="SW_R", inc_amt=16)
        M_RT = 128

        build_done = {}
        for e_i in range(NE):
            sem = f"SWb{e_i}d"
            m = sch.op("gpsimd", lambda e, e_i=e_i: e.dma_start(
                out=wdc[e_i].rearrange("fc p dd -> (fc p) dd"), in_=wd_h[e_i]),
                inc=sem, inc_amt=16)
            build_done[(e_i, "d")] = (sem, m)
            for mat, src_h, dst_h in (("g", wg_h, wgc), ("u", wu_h, wuc)):
                sem = f"SWb{e_i}{mat}"
                m = None
                for dc in range(NDC):
                    for q in range(4):
                        fc0, fc1 = q * 4, (q + 1) * 4
                        m = sch.op("gpsimd", lambda e, e_i=e_i, dc=dc, fc0=fc0, fc1=fc1,
                                   src_h=src_h, dst_h=dst_h: e.dma_start(
                            out=dst_h[e_i, fc0:fc1, :, dc * 128:(dc + 1) * 128].rearrange("fc p f -> p fc f"),
                            in_=src_h[e_i, dc * 128:(dc + 1) * 128, fc0 * 128:fc1 * 128].rearrange("p (fc f) -> p fc f", f=128)),
                            inc=sem, inc_amt=16)
                build_done[(e_i, mat)] = (sem, m)

        rs16 = lstat[:, 0:16]
        ssq16 = lstat[:, 16:32]
        mean16 = lstat[:, 32:48]
        ssqm16 = lstat[:, 48:64]
        lm2 = lstat[:, 64:80]
        lm2b = lstat[:, 80:96]
        lvar = lstat[:, 96:112]
        lsd = lstat[:, 112:128]
        lri = lstat[:, 128:144]
        lri2 = lstat[:, 144:160]
        ln_ssq = {}
        hoisted_ri2 = {}

        def emit_L1_tile(sq, tc):
            p = tc % 2
            sb = sq * S
            w_ln = []
            if sq > 0 and tc < 2:
                w_ln.append((SV, xn_last_prev))
            if tc >= 2:
                w_ln.append((SV, ln_ssq[(sq, tc - 2)]))
            ldq = "sync" if sq == 0 else "gpsimd"
            m_x = sch.op(ldq, lambda e, tc=tc, p=p, sb=sb: e.dma_start(
                out=x_t[p], in_=x_h[sb + tc * 128: sb + (tc + 1) * 128, :]),
                waits=w_ln, inc=f"SX_X{p}", inc_amt=16)
            sch.op("vector", lambda e, tc=tc, p=p: e.reduce_sum(
                rs16[:, tc:tc + 1], x_t[p], axis=AX.X), waits=[(f"SX_X{p}", m_x)])
            sch.op("vector", lambda e, p=p: e.tensor_tensor(
                sqdump, x_t[p], x_t[p], op=ALU.mult))
            ln_ssq[(sq, tc)] = sch.op("vector", lambda e, tc=tc: e.reduce_sum(
                ssq16[:, tc:tc + 1], sqdump, axis=AX.X), inc=SV)

        def emit_L2(sq):
            sch.op("scalar", lambda e: e.mul(mean16, rs16, 1.0 / D), waits=[(SV, ln_ssq[(sq, LNT - 1)])])
            m_l2 = sch.op("scalar", lambda e: e.mul(ssqm16, ssq16, 1.0 / D), inc=SS)
            m_m2 = sch.op("vector", lambda e: e.tensor_tensor(lm2, mean16, mean16, op=ALU.mult),
                          waits=[(SS, m_l2)], inc=SV)
            m_m2b = sch.op("scalar", lambda e: e.copy(lm2b, lm2), waits=[(SV, m_m2)], inc=SS)
            m_var = sch.op("vector", lambda e: e.scalar_tensor_tensor(
                lvar, ssqm16, 1.0, lm2b, op0=ALU.mult, op1=ALU.subtract),
                waits=[(SS, m_m2b)], inc=SV)
            m_sdm = sch.op("scalar", lambda e: e.activation(lsd, lvar, AF.Sqrt, bias=np.float32(EPS)),
                           waits=[(SV, m_var)], inc=SS)
            m_ri = sch.op("vector", lambda e: e.reciprocal(lri, lsd), waits=[(SS, m_sdm)], inc=SV)
            m_ri2 = sch.op("scalar", lambda e: e.copy(lri2, lri), waits=[(SV, m_ri)], inc=SS)
            if debug:
                sch.op("scalar", lambda e: e.copy(dls[:, 0:16], mean16))
                sch.op("scalar", lambda e: e.copy(dls[:, 32:48], lri2))
            return m_ri2

        for seq in range(SEQ_PER_CORE):
            sbase = seq * S
            if seq in hoisted_ri2:
                m_ri2 = hoisted_ri2[seq]
            else:
                for tc in range(LNT):
                    emit_L1_tile(seq, tc)
                m_ri2 = emit_L2(seq)
            m_ssq = {tc: ln_ssq[(seq, tc)] for tc in range(LNT)}

            xn_done = None
            m_xn = {}
            m_lc = None
            for tc in range(LNT):
                p = tc % 2
                w3 = [(SV, m_ssq[LNT - 2 + p])] if tc < 2 else [(SV, m_xn[tc - 2])]
                ldq = "sync" if seq == 0 else "gpsimd"
                m_x = sch.op(ldq, lambda e, tc=tc, p=p, sbase=sbase: e.dma_start(
                    out=x_t[p], in_=x_h[sbase + tc * 128: sbase + (tc + 1) * 128, :]),
                    waits=w3, inc=f"SX_X{p}", inc_amt=16)
                w_xn = [(f"SX_X{p}", m_x), (SS, m_ri2)]
                if seq > 0 and tc == 0:
                    w_xn.append((ST, xn_guard))
                if debug and seq > 0 and tc == 0:
                    w_xn.append(("SX_B", 64 * seq))
                m_xn[tc] = sch.op("vector", lambda e, p=p, tc=tc: e.tensor_scalar(
                    xn_tm[:, tc * D:(tc + 1) * D], x_t[p], mean16[:, tc:tc + 1], lri2[:, tc:tc + 1],
                    op0=ALU.subtract, op1=ALU.mult), waits=w_xn, inc=SV)
                tb = f"T{p}"
                m_tr = None
                for dc in range(NDC):
                    m_tr = sch.op("tensor", lambda e, p=p, tc=tc, dc=dc: e.transpose(
                        T[p][:, dc * 128:(dc + 1) * 128],
                        xn_tm[:, tc * D + dc * 128: tc * D + (dc + 1) * 128], ident[:]),
                        waits=([(SV, m_xn[tc]), ("SX_C", M_C)] + (bwait(tb) if dc == 0 else [])),
                        inc=(ST if dc == NDC - 1 else None))
                m_ev = sch.op("vector", lambda e, p=p, tc=tc: e.tensor_copy(
                    xn_fm[:].rearrange("p (dc t) -> p dc t", dc=NDC)[:, :, tc * 128:(tc + 1) * 128],
                    T[p][:].rearrange("p (dc t) -> p dc t", dc=NDC)),
                    waits=[(ST, m_tr)], inc=SV)
                bankfree[tb] = (SV, m_ev)
                xn_done = m_ev
                m_lg = None
                for dc in range(NDC):
                    m_lg = sch.op("tensor", lambda e, tc=tc, dc=dc: e.matmul(
                        P[4][:, 0:NE],
                        xn_fm[:, dc * S + tc * 128: dc * S + (tc + 1) * 128],
                        rt_bf[:, dc * NE:(dc + 1) * NE],
                        start=(dc == 0), stop=(dc == NDC - 1)),
                        waits=([(SV, m_ev), ("SW_R", M_RT)] + (bwait("P4") if dc == 0 else [])),
                        inc=(ST if dc == NDC - 1 else None))
                m_lc = sch.op("scalar", lambda e, tc=tc: e.copy(
                    logits_sb[:, tc * NE:(tc + 1) * NE], P[4][:, 0:NE]),
                    waits=[(ST, m_lg)], inc=SS)
                bankfree["P4"] = (SS, m_lc)
            xn_last_prev = m_xn[LNT - 1]

            def a3(t):
                return t[:].rearrange("p (t e) -> p t e", e=NE)

            def bc16(ap16):
                return ap16.unsqueeze(2).broadcast_to([128, 16, NE])

            def bc8(ap8):
                return ap8.unsqueeze(1).broadcast_to([128, 16, NE])

            nmax = r16[:, 0:16]
            nmaxb = r16[:, 16:32]
            rsum = r16[:, 32:48]
            rsumb = r16[:, 48:64]
            rrec = r16[:, 64:80]
            rrecb = r16[:, 80:96]
            m1 = r16[:, 96:112]
            m1b = r16[:, 112:128]
            m2 = r16[:, 128:144]
            m2b = r16[:, 144:160]
            c1 = r16[:, 160:176]
            c1b = r16[:, 176:192]
            c2 = r16[:, 192:208]
            c2b = r16[:, 208:224]
            d21 = r16[:, 224:240]
            nw1 = r16[:, 240:256]
            nw2 = r16[:, 256:272]
            mv = sch.op("vector", lambda e: e.tensor_reduce(nmax, a3(logits_sb), axis=AX.X, op=ALU.max, negate=True),
                        waits=[(SS, m_lc)], inc=SV)
            ms = sch.op("scalar", lambda e: e.copy(nmaxb, nmax), waits=[(SV, mv)], inc=SS)
            mv = sch.op("vector", lambda e: e.tensor_tensor(a3(rb), a3(logits_sb), bc16(nmaxb), op=ALU.add),
                        waits=[(SS, ms)], inc=SV)
            ms = sch.op("scalar", lambda e: e.activation(ra[:], rb[:], AF.Exp), waits=[(SV, mv)], inc=SS)
            mv = sch.op("vector", lambda e: e.tensor_reduce(rsum, a3(ra), axis=AX.X, op=ALU.add),
                        waits=[(SS, ms)], inc=SV)
            ms = sch.op("scalar", lambda e: e.copy(rsumb, rsum), waits=[(SV, mv)], inc=SS)
            mv = sch.op("vector", lambda e: e.reciprocal(rrec, rsumb), waits=[(SS, ms)], inc=SV)
            ms = sch.op("scalar", lambda e: e.copy(rrecb, rrec), waits=[(SV, mv)], inc=SS)
            sch.op("vector", lambda e: e.tensor_tensor(a3(rb), a3(ra), bc16(rrecb), op=ALU.mult),
                   waits=[(SS, ms)])
            mv = sch.op("vector", lambda e: e.tensor_reduce(m1, a3(rb), axis=AX.X, op=ALU.max), inc=SV)
            ms = sch.op("scalar", lambda e: e.copy(m1b, m1), waits=[(SV, mv)], inc=SS)
            sch.op("vector", lambda e: e.tensor_tensor(a3(rc), a3(rb), bc16(m1b), op=ALU.is_ge),
                   waits=[(SS, ms)])
            sch.op("vector", lambda e: e.tensor_tensor(a3(rd), a3(rc), bc8(c8[:]), op=ALU.mult))
            mv = sch.op("vector", lambda e: e.tensor_reduce(c1, a3(rd), axis=AX.X, op=ALU.max), inc=SV)
            ms = sch.op("scalar", lambda e: e.copy(c1b, c1), waits=[(SV, mv)], inc=SS)
            sch.op("vector", lambda e: e.tensor_tensor(a3(rc), a3(rd), bc16(c1b), op=ALU.is_ge),
                   waits=[(SS, ms)])
            sch.op("vector", lambda e: e.scalar_tensor_tensor(a3(rd), a3(rc), -2.0, a3(rb), op0=ALU.mult, op1=ALU.add))
            mv = sch.op("vector", lambda e: e.tensor_reduce(m2, a3(rd), axis=AX.X, op=ALU.max), inc=SV)
            ms = sch.op("scalar", lambda e: e.copy(m2b, m2), waits=[(SV, mv)], inc=SS)
            sch.op("vector", lambda e: e.tensor_tensor(a3(ra), a3(rd), bc16(m2b), op=ALU.is_ge),
                   waits=[(SS, ms)])
            sch.op("vector", lambda e: e.tensor_tensor(a3(rd), a3(ra), bc8(c8[:]), op=ALU.mult))
            mv = sch.op("vector", lambda e: e.tensor_reduce(c2, a3(rd), axis=AX.X, op=ALU.max), inc=SV)
            ms = sch.op("scalar", lambda e: e.copy(c2b, c2), waits=[(SV, mv)], inc=SS)
            sch.op("vector", lambda e: e.tensor_tensor(a3(ra), a3(rd), bc16(c2b), op=ALU.is_ge),
                   waits=[(SS, ms)])
            mv = sch.op("vector", lambda e: e.tensor_tensor(d21, m2b, m1b, op=ALU.subtract), inc=SV)
            sch.op("scalar", lambda e: e.activation(nw2, d21, AF.Sigmoid), waits=[(SV, mv)])
            ms = sch.op("scalar", lambda e: e.activation(nw1, d21, AF.Sigmoid, scale=np.float32(-1.0)), inc=SS)
            sch.op("vector", lambda e: e.tensor_tensor(a3(rb), a3(rc), bc16(nw1), op=ALU.mult),
                   waits=[(SS, ms)])
            sch.op("vector", lambda e: e.tensor_tensor(a3(rd), a3(ra), bc16(nw2), op=ALU.mult))
            m_fw = sch.op("vector", lambda e: e.tensor_tensor(fw_sb[:], rb[:], rd[:], op=ALU.add), inc=SV)
            sg_busy = [xn_done, xn_done]

            if debug:
                sch.op("sync", lambda e, sbase=sbase: e.dma_start(
                    out=dbg_xn[sbase:sbase + S, :].rearrange("(tc p) d -> p tc d", p=128),
                    in_=xn_tm[:].rearrange("p (tc d) -> p tc d", tc=LNT)),
                    waits=[(SV, xn_done)], inc="SX_B", inc_amt=16)
                sch.op("sync", lambda e, seq=seq: e.dma_start(
                    out=dbg_fw[seq], in_=fw_sb[:]), waits=[(SV, m_fw)], inc="SX_B", inc_amt=16)
                sch.op("sync", lambda e, seq=seq: e.dma_start(
                    out=dbg_ls[seq], in_=dls[:]), inc="SX_B", inc_amt=16)
                sch.op("sync", lambda e, seq=seq: e.dma_start(
                    out=dbg_xf[seq], in_=xn_fm[:]), inc="SX_B", inc_amt=16)

            for tg in range(NTG):
                tgg = seq * NTG + tg
                tbase = sbase + tg * TG
                m_ai = None
                for tc in range(NTC):
                    w_ai = [(SO, store_mark.get(tgg - 1))] if tc == 0 else []
                    m_ai = sch.op("sync", lambda e, tc=tc, tbase=tbase: e.dma_start(
                        out=acc[:, tc * D:(tc + 1) * D],
                        in_=x_h[tbase + tc * 128: tbase + (tc + 1) * 128, :]),
                        waits=w_ai, inc="SX_A", inc_amt=16)

                at = {}
                for qc in range(NTC + 1):
                    if qc < NTC:
                        qt = tg * NTC + qc
                        pp = qc % 2
                        m_xw = None
                        for dc in range(NDC):
                            m_xw = sch.op("vector", lambda e, dc=dc, pp=pp, qt=qt: e.tensor_scalar(
                                xw_buf[pp][:, dc * 128:(dc + 1) * 128],
                                xn_fm[:, dc * S + qt * 128: dc * S + (qt + 1) * 128],
                                wlr[:, dc:dc + 1], 0.0, op0=ALU.mult, op1=ALU.add),
                                waits=[(SV, m_wlr), (SV, xn_done)],
                                inc=(SV if dc == NDC - 1 else None))
                        m_sc = None
                        for dc in range(NDC):
                            for kt in range(4):
                                m_sc = sch.op("tensor", lambda e, dc=dc, kt=kt, pp=pp: e.matmul(
                                    P[kt][:],
                                    xw_buf[pp][:, dc * 128:(dc + 1) * 128],
                                    xn_fm[:, dc * S + kt * 512: dc * S + (kt + 1) * 512],
                                    start=(dc == 0), stop=(dc == NDC - 1)),
                                    waits=([(SV, m_xw)] + (bwait("P0", "P1", "P2", "P3") if dc == 0 else [])),
                                    inc=(ST if (dc == NDC - 1 and kt == 3) else None))
                        at[(qc, "sc")] = m_sc
                    if qc >= 1:
                        q1 = qc - 1
                        m_tr = [None, None]
                        for half in range(2):
                            for j in range(8):
                                kc = half * 8 + j
                                m_tr[half] = sch.op("tensor", lambda e, half=half, j=j, kc=kc: e.transpose(
                                    T[half][:, j * 128:(j + 1) * 128],
                                    attn_sb[:, kc * 128:(kc + 1) * 128], ident[:]),
                                    waits=([(SS, at[(q1, "exp")])] + (bwait(f"T{half}") if j == 0 else [])),
                                    inc=(ST if j == 7 else None))
                        at[(q1, "tr")] = m_tr
                    if qc < NTC:
                        m_m4 = None
                        for kt in range(4):
                            m_m4 = sch.op("vector", lambda e, kt=kt: e.reduce_max(
                                astat[:, kt:kt + 1], P[kt][:], axis=AX.X), waits=[(ST, at[(qc, "sc")])],
                                inc=(SV if kt == 3 else None))
                        m_m4b = sch.op("scalar", lambda e: e.copy(astat[:, 4:8], astat[:, 0:4]),
                                       waits=[(SV, m_m4)], inc=SS)
                        at[(qc, "nm")] = sch.op("vector", lambda e: e.reduce_max(
                            astat[:, 8:9], astat[:, 4:8], axis=AX.X, negate=True),
                            waits=[(SS, m_m4b)], inc=SV)
                    if qc >= 1:
                        q1 = qc - 1
                        m_tr = at[(q1, "tr")]
                        m_te = [None, None]
                        for half in range(2):
                            m_te[half] = sch.op("vector", lambda e, half=half: e.tensor_copy(
                                attnT_sb[:, half * 1024:(half + 1) * 1024], T[half][:]),
                                waits=[(ST, m_tr[half])], inc=SV)
                            bankfree[f"T{half}"] = (SV, m_te[half])
                        m_av = None
                        for kc in range(16):
                            for dh in range(2):
                                m_av = sch.op("tensor", lambda e, kc=kc, dh=dh: e.matmul(
                                    P[4 + dh][:],
                                    attnT_sb[:, kc * 128:(kc + 1) * 128],
                                    xn_tm[:, kc * D + dh * 512: kc * D + (dh + 1) * 512],
                                    start=(kc == 0), stop=(kc == 15)),
                                    waits=([(SV, m_te[kc // 8])] + (bwait("P4", "P5") if kc == 0 else [])),
                                    inc=(ST if (kc == 15 and dh == 1) else None))
                        at[(q1, "av")] = m_av
                    if qc < NTC:
                        m_exp = None
                        w_exp0 = [(ST, at[(qc - 1, "tr")][1])] if qc >= 1 else []
                        for kt in range(4):
                            m_exp = sch.op("scalar", lambda e, kt=kt: e.activation(
                                attn_sb[:, kt * 512:(kt + 1) * 512], P[kt][:], AF.Exp,
                                bias=astat[:, 8:9], accum_out=astat[:, 9 + kt:10 + kt]),
                                waits=[(SV, at[(qc, "nm")])] + (w_exp0 if kt == 0 else []),
                                inc=(SS if kt == 3 else None))
                        at[(qc, "exp")] = m_exp
                        for i in range(4):
                            bankfree[f"P{i}"] = (SS, m_exp)
                        rp = qc % 2
                        m_su = sch.op("vector", lambda e: e.reduce_sum(
                            astat[:, 13:14], astat[:, 9:13], axis=AX.X), waits=[(SS, m_exp)], inc=SV)
                        m_sb = sch.op("scalar", lambda e: e.copy(astat[:, 14:15], astat[:, 13:14]),
                                      waits=[(SV, m_su)], inc=SS)
                        m_rc2 = sch.op("vector", lambda e, rp=rp: e.reciprocal(
                            astat[:, 15 + 2 * rp:16 + 2 * rp], astat[:, 14:15]),
                            waits=[(SS, m_sb)], inc=SV)
                        at[(qc, "recb")] = sch.op("scalar", lambda e, rp=rp: e.copy(
                            astat[:, 16 + 2 * rp:17 + 2 * rp], astat[:, 15 + 2 * rp:16 + 2 * rp]),
                            waits=[(SV, m_rc2)], inc=SS)
                    if qc >= 1:
                        q1 = qc - 1
                        rp = q1 % 2
                        m_ae = None
                        for dh in range(2):
                            m_ae = sch.op("vector", lambda e, dh=dh, q1=q1, rp=rp: e.scalar_tensor_tensor(
                                acc[:, q1 * D + dh * 512: q1 * D + (dh + 1) * 512],
                                P[4 + dh][:], astat[:, 16 + 2 * rp:17 + 2 * rp],
                                acc[:, q1 * D + dh * 512: q1 * D + (dh + 1) * 512],
                                op0=ALU.mult, op1=ALU.add),
                                waits=[(ST, at[(q1, "av")]), (SS, at[(q1, "recb")]), ("SX_A", m_ai)], inc=SV)
                        bankfree["P4"] = (SV, m_ae)
                        bankfree["P5"] = (SV, m_ae)

                pending_down = None

                def emit_down(e_d, m_hm_d, m_wd_d, odd_pair):
                    nonlocal h_guard, wd_free, xn_guard, acc_done
                    m_dn_last = None
                    m_de = None
                    for tc in range(NTC):
                        dpair = (4, 5) if tc % 2 == 0 else odd_pair
                        m_d = None
                        for fc in range(NFC):
                            for ddh in range(2):
                                m_d = sch.op("tensor", lambda e, fc=fc, ddh=ddh, tc=tc, dpair=dpair: e.matmul(
                                    P[dpair[ddh]][:],
                                    h_sb[:, fc * TG + tc * 128: fc * TG + (tc + 1) * 128],
                                    wd_all[:, fc * D + ddh * 512: fc * D + (ddh + 1) * 512],
                                    start=(fc == 0), stop=(fc == NFC - 1)),
                                    waits=([(SV, m_hm_d), ("SX_D", m_wd_d)] + (bwait(f"P{dpair[0]}", f"P{dpair[1]}") if fc == 0 else [])),
                                    inc=(ST if (fc == NFC - 1 and ddh == 1) else None))
                        m_dn_last = m_d
                        fcol = (tg * NTC + tc) * NE + e_d
                        for ddh in range(2):
                            m_de = sch.op("vector", lambda e, ddh=ddh, tc=tc, dpair=dpair, fcol=fcol: e.scalar_tensor_tensor(
                                acc[:, tc * D + ddh * 512: tc * D + (ddh + 1) * 512],
                                P[dpair[ddh]][:], fw_sb[:, fcol:fcol + 1],
                                acc[:, tc * D + ddh * 512: tc * D + (ddh + 1) * 512],
                                op0=ALU.mult, op1=ALU.add),
                                waits=[(ST, m_d), (SV, m_fw)], inc=SV)
                        bankfree[f"P{dpair[0]}"] = (SV, m_de)
                        bankfree[f"P{dpair[1]}"] = (SV, m_de)
                    h_guard = m_dn_last
                    wd_free = m_dn_last
                    xn_guard = m_dn_last
                    acc_done = m_de

                for e_i in range(NE):
                    first_of_tg = (pending_down is None)
                    m_wg = {}
                    m_wu = {}

                    def load_wg(fc):
                        pp = fc % 2
                        m_wg[fc] = sch.op("sync", lambda e, e_i=e_i, fc=fc, pp=pp: e.dma_start(
                            out=wg_buf[pp][:], in_=wgc[e_i, fc]),
                            waits=[build_done[(e_i, "g")], (ST, wg_busy[pp])],
                            inc=f"SX_G{pp}", inc_amt=16)

                    def load_wu(fc):
                        pp = fc % 2
                        m_wu[fc] = sch.op("sync", lambda e, e_i=e_i, fc=fc, pp=pp: e.dma_start(
                            out=wu_buf[pp][:], in_=wuc[e_i, fc]),
                            waits=[build_done[(e_i, "u")], (ST, wu_busy[pp])],
                            inc=f"SX_U{pp}", inc_amt=16)

                    def load_wd():
                        m = None
                        for fc in range(NFC):
                            m = sch.op("sync", lambda e, e_i=e_i, fc=fc: e.dma_start(
                                out=wd_all[:, fc * D:(fc + 1) * D], in_=wdc[e_i, fc]),
                                waits=[build_done[(e_i, "d")]] + ([(ST, wd_free)] if fc == 0 else []),
                                inc="SX_D", inc_amt=16)
                        return m

                    for fc in (0, 1):
                        load_wg(fc)
                        load_wu(fc)
                    if pending_down is None:
                        m_wd = load_wd()

                    m_hm_last = None
                    for fc in range(NFC):
                        pp = fc % 2
                        gpair = (0, 1)
                        upair = (2, 3)
                        m_g = None
                        for dc in range(NDC):
                            for tt in range(2):
                                m_g = sch.op("tensor", lambda e, dc=dc, tt=tt, pp=pp, gpair=gpair, tg=tg: e.matmul(
                                    P[gpair[tt]][:],
                                    wg_buf[pp][:, dc * 128:(dc + 1) * 128],
                                    xn_fm[:, dc * S + tg * TG + tt * 512: dc * S + tg * TG + (tt + 1) * 512],
                                    start=(dc == 0), stop=(dc == NDC - 1)),
                                    waits=([(f"SX_G{pp}", m_wg[fc])] + (bwait(f"P{gpair[0]}", f"P{gpair[1]}") if dc == 0 else [])),
                                    inc=(ST if (dc == NDC - 1 and tt == 1) else None))
                        wg_busy[pp] = m_g
                        if fc + 2 < NFC:
                            load_wg(fc + 2)
                        m_si = None
                        for tt in range(2):
                            m_si = sch.op("scalar", lambda e, tt=tt, pp=pp, gpair=gpair: e.activation(
                                sg_t[pp][:, tt * 512:(tt + 1) * 512], P[gpair[tt]][:], AF.Silu),
                                waits=[(ST, m_g), (SV, sg_busy[pp])],
                                inc=(SS if tt == 1 else None))
                        bankfree[f"P{gpair[0]}"] = (SS, m_si)
                        bankfree[f"P{gpair[1]}"] = (SS, m_si)
                        m_u = None
                        for dc in range(NDC):
                            for tt in range(2):
                                m_u = sch.op("tensor", lambda e, dc=dc, tt=tt, pp=pp, upair=upair, tg=tg: e.matmul(
                                    P[upair[tt]][:],
                                    wu_buf[pp][:, dc * 128:(dc + 1) * 128],
                                    xn_fm[:, dc * S + tg * TG + tt * 512: dc * S + tg * TG + (tt + 1) * 512],
                                    start=(dc == 0), stop=(dc == NDC - 1)),
                                    waits=([(f"SX_U{pp}", m_wu[fc])] + (bwait(f"P{upair[0]}", f"P{upair[1]}") if dc == 0 else [])),
                                    inc=(ST if (dc == NDC - 1 and tt == 1) else None))
                        wu_busy[pp] = m_u
                        if fc + 2 < NFC:
                            load_wu(fc + 2)
                        if fc == 0 and pending_down is not None:
                            pending_down()
                        m_hm = None
                        w_h = [(ST, h_guard)] if (h_guard is not None and fc == 0) else []
                        for tt in range(2):
                            m_hm = sch.op("vector", lambda e, tt=tt, pp=pp, fc=fc, upair=upair: e.tensor_tensor(
                                h_sb[:, fc * TG + tt * 512: fc * TG + (tt + 1) * 512],
                                sg_t[pp][:, tt * 512:(tt + 1) * 512], P[upair[tt]][:], op=ALU.mult),
                                waits=[(SS, m_si), (ST, m_u)] + (w_h if tt == 0 else []),
                                inc=(SV if tt == 1 else None))
                        sg_busy[pp] = m_hm
                        bankfree[f"P{upair[0]}"] = (SV, m_hm)
                        bankfree[f"P{upair[1]}"] = (SV, m_hm)
                        m_hm_last = m_hm

                    if not first_of_tg:
                        m_wd = load_wd()
                    pending_down = (lambda e_d=e_i, m_hm_d=m_hm_last, m_wd_d=m_wd:
                                    emit_down(e_d, m_hm_d, m_wd_d, (0, 1)))
                    if tg == NTG - 1 and seq + 1 < SEQ_PER_CORE:
                        emit_L1_tile(seq + 1, 2 * e_i)
                        emit_L1_tile(seq + 1, 2 * e_i + 1)
                        if e_i == NE - 1:
                            hoisted_ri2[seq + 1] = emit_L2(seq + 1)

                emit_down(NE - 1, m_hm_last, m_wd, (2, 3))
                pending_down = None

                m_st = None
                for tc in range(NTC):
                    m_st = sch.op("sync", lambda e, tc=tc, tbase=tbase: e.dma_start(
                        out=out_h[tbase + tc * 128: tbase + (tc + 1) * 128, :],
                        in_=acc[:, tc * D:(tc + 1) * D]),
                        waits=[(SV, acc_done)], inc=SO, inc_amt=16)
                store_mark[tgg] = m_st

        total_so = sch.cnt[SO]
        sch.op("sync", lambda e: e.nop(), waits=[(SO, total_so)])

        with nc.Block() as block:
            for eng_name in ENGINES:
                ops = sch.ops[eng_name]

                def make(ops):
                    def emitter(eng):
                        waited = {}
                        for fn, waits, inc, inc_amt in ops:
                            for s, v in waits:
                                if waited.get(s, 0) < v:
                                    eng.wait_ge(sems[s], v)
                                    waited[s] = v
                            inst = fn(eng)
                            if inc is not None:
                                inst.then_inc(sems[inc], inc_amt)
                    return emitter

                getattr(block, eng_name)(make(ops))

    return nc


_NC_CACHE = {}


def _get_nc(debug=False):
    if debug not in _NC_CACHE:
        _NC_CACHE[debug] = build_nc(debug)
    return _NC_CACHE[debug]


def make_inputs(L0_attn_x, L0_attn_w_lhs, L0_attn_w_rhs, L0_ffn_router_kernel_DE,
                L0_ffn_kernel_gating_EDF, L0_ffn_kernel_up_EDF, L0_ffn_kernel_down_EFD):
    x = np.ascontiguousarray(np.asarray(L0_attn_x, dtype=np.float32))
    wl = np.ascontiguousarray(np.asarray(L0_attn_w_lhs, dtype=np.float32).reshape(NDC, 128).T)
    wr = np.ascontiguousarray(np.asarray(L0_attn_w_rhs, dtype=np.float32).reshape(NDC, 128).T)
    rt = np.ascontiguousarray(np.asarray(L0_ffn_router_kernel_DE, dtype=np.float32))
    wg = np.ascontiguousarray(np.asarray(L0_ffn_kernel_gating_EDF, dtype=np.float32))
    wu = np.ascontiguousarray(np.asarray(L0_ffn_kernel_up_EDF, dtype=np.float32))
    wd = np.ascontiguousarray(np.asarray(L0_ffn_kernel_down_EFD, dtype=np.float32))
    ident = np.eye(128, dtype=ml_dtypes.bfloat16)
    c8 = np.broadcast_to((NE - np.arange(NE, dtype=np.float32)), (128, NE)).copy()
    in_maps = []
    for c in range(NCORES):
        xs = x[c * SEQ_PER_CORE:(c + 1) * SEQ_PER_CORE].reshape(TOKENS, D)
        in_maps.append(dict(x=np.ascontiguousarray(xs), wl=wl, wr=wr, rt=rt,
                            wg=wg, wu=wu, wd=wd, ident=ident, c8=c8))
    return in_maps


def kernel(L0_attn_x, L0_attn_w_lhs, L0_attn_w_rhs, L0_ffn_router_kernel_DE,
           L0_ffn_kernel_gating_EDF, L0_ffn_kernel_up_EDF, L0_ffn_kernel_down_EFD):
    in_maps = make_inputs(L0_attn_x, L0_attn_w_lhs, L0_attn_w_rhs, L0_ffn_router_kernel_DE,
                          L0_ffn_kernel_gating_EDF, L0_ffn_kernel_up_EDF, L0_ffn_kernel_down_EFD)
    nc = _get_nc(debug=False)
    res = run_bass_kernel_spmd(nc, in_maps, core_ids=list(range(NCORES)))
    out = np.concatenate([np.asarray(r["out"], dtype=np.float32).reshape(SEQ_PER_CORE, S, D)
                          for r in res.results], axis=0)
    return out
